# version 08
# baseline (speedup 1.0000x reference)
import numpy as np
import concourse.bass as bass
import concourse.mybir as mybir
from concourse.bass_utils import run_bass_kernel_spmd

F32 = mybir.dt.float32
BF16 = mybir.dt.bfloat16
AF = mybir.ActivationFunctionType
ALU = mybir.AluOpType
AXX = mybir.AxisListType.X

HID = 4096
S = 4096
NHC = 16
NCC = 48
NDC = 32
TW = 512
EPS = 1e-5
NOWN = 2048


class Gd:
    __slots__ = ("sem", "val")

    def __init__(self, sem=None, val=0):
        self.sem = sem
        self.val = val

    def snap(self):
        return Gd(self.sem, self.val)


class _EngProxy:
    _SKIP = {"wait_ge", "nop", "maybe_wait_ge", "sem_inc", "br", "dma_start",
             "dma_start_transpose", "collective_compute", "switch_core_id"}

    def __init__(self, e, barsem=None):
        object.__setattr__(self, "_e", e)
        object.__setattr__(self, "last", None)
        object.__setattr__(self, "barsem", barsem)
        object.__setattr__(self, "barcount", 0)

    def __getattr__(self, k):
        a = getattr(object.__getattribute__(self, "_e"), k)
        if callable(a) and k not in _EngProxy._SKIP and not k.startswith("_"):
            def wrap(*args, **kw):
                r = a(*args, **kw)
                if hasattr(r, "then_inc"):
                    object.__setattr__(self, "last", r)
                return r
            return wrap
        return a

    def bar(self):
        last = object.__getattribute__(self, "last")
        sem = object.__getattribute__(self, "barsem")
        assert last is not None and sem is not None
        cnt = object.__getattribute__(self, "barcount") + 1
        object.__setattr__(self, "barcount", cnt)
        last.then_inc(sem, 1)
        object.__setattr__(self, "last", None)
        object.__getattribute__(self, "_e").wait_ge(sem, cnt)

    def fin(self, sem, n=1):
        last = object.__getattribute__(self, "last")
        assert last is not None, "fin() with no preceding datapath op"
        last.then_inc(sem, n)
        object.__setattr__(self, "last", None)


def build_program(dbg=0, phases="ABCD"):
    nc = bass.Bass("TRN2", target_bir_lowering=False)

    xT = nc.dram_tensor("xT", [HID, S], F32, kind="ExternalInput").ap()
    xTown = nc.dram_tensor("xTown", [HID, NOWN], F32, kind="ExternalInput").ap()
    wQKV = nc.dram_tensor("wQKV", [NCC, HID, 128], F32, kind="ExternalInput").ap()
    bQKVpm = nc.dram_tensor("bQKVpm", [128, NCC], F32, kind="ExternalInput").ap()
    wP = nc.dram_tensor("wP", [HID, HID], F32, kind="ExternalInput").ap()
    bPpm = nc.dram_tensor("bPpm", [128, NDC], F32, kind="ExternalInput").ap()
    wFT = nc.dram_tensor("wFT", [NDC, HID, 128], F32, kind="ExternalInput").ap()
    bFpm = nc.dram_tensor("bFpm", [128, NDC], F32, kind="ExternalInput").ap()
    aTi = nc.dram_tensor("aTi", [HID, 128], F32, kind="ExternalInput").ap()
    bBi = nc.dram_tensor("bBi", [128, HID], F32, kind="ExternalInput").ap()
    wRT = nc.dram_tensor("wRT", [HID, 8], F32, kind="ExternalInput").ap()
    bRi = nc.dram_tensor("bRi", [1, 8], F32, kind="ExternalInput").ap()
    kveci = nc.dram_tensor("kveci", [128, NDC], F32, kind="ExternalInput").ap()
    qveci = nc.dram_tensor("qveci", [1, S], F32, kind="ExternalInput").ap()
    slpi = nc.dram_tensor("slpi", [128, NHC], F32, kind="ExternalInput").ap()
    nslpi = nc.dram_tensor("nslpi", [1, NHC], F32, kind="ExternalInput").ap()
    mdeli = nc.dram_tensor("mdeli", [4, 128, TW], F32, kind="ExternalInput").ap()
    er8i = nc.dram_tensor("er8i", [8, 128], F32, kind="ExternalInput").ap()
    rmaski = nc.dram_tensor("rmaski", [1, NOWN], F32, kind="ExternalInput").ap()

    outT = nc.dram_tensor("outT", [HID, NOWN], F32, kind="ExternalOutput").ap()

    qkvT = nc.dram_tensor("qkvT", [3 * 2048, S], BF16, kind="Internal").ap()
    ag_in = nc.dram_tensor("ag_in", [2048, S], BF16, kind="Internal")
    ag_out = nc.dram_tensor("ag_out", [HID, S], BF16, kind="Internal")
    rq_in = nc.dram_tensor("rq_in", [128, NDC], F32, kind="Internal")
    rq_out = nc.dram_tensor("rq_out", [128, NDC], F32, kind="Internal")
    r8dram = nc.dram_tensor("r8dram", [1, 8], BF16, kind="Internal").ap()
    x2sc = nc.dram_tensor("x2sc", [HID, NOWN], F32, kind="Internal").ap()
    wPbf = nc.dram_tensor("wPbf", [HID, HID], BF16, kind="Internal").ap()
    wFbf = nc.dram_tensor("wFbf", [NDC, HID, 128], BF16, kind="Internal").ap()
    x3sc = nc.dram_tensor("x3sc", [HID, NOWN], BF16, kind="Internal").ap()
    mu1sc = nc.dram_tensor("mu1sc", [1, S], F32, kind="Internal").ap()

    dbg_outs = {}
    if dbg:
        dbg_outs["dqkv"] = nc.dram_tensor("dqkv", [3 * 2048, S], BF16, kind="ExternalOutput").ap()
        dbg_outs["dag"] = nc.dram_tensor("dag", [HID, S], BF16, kind="ExternalOutput").ap()
        dbg_outs["dx2"] = nc.dram_tensor("dx2", [HID, NOWN], F32, kind="ExternalOutput").ap()
        dbg_outs["dx3"] = nc.dram_tensor("dx3", [HID, NOWN], BF16, kind="ExternalOutput").ap()
        dbg_outs["drt"] = nc.dram_tensor("drt", [128, 1], F32, kind="ExternalOutput").ap()

    Q = {k: [] for k in ("sync", "gpsimd", "tensor", "vector", "scalar")}

    def em(eng, fn):
        Q[eng].append(fn)

    sems = {}
    val = {}

    import contextlib
    with contextlib.ExitStack() as _ES:
        AR = _ES.enter_context(nc.sbuf_tensor("AR", [128, 46080], F32))
        kvec_s = _ES.enter_context(nc.sbuf_tensor("kvec_s", [128, NDC], F32))
        slp_s = _ES.enter_context(nc.sbuf_tensor("slp_s", [128, NHC], F32))
        nslp_s = _ES.enter_context(nc.sbuf_tensor("nslp_s", [1, NHC], F32))
        bqkv_s = _ES.enter_context(nc.sbuf_tensor("bqkv_s", [128, NCC], F32))
        bp_s = _ES.enter_context(nc.sbuf_tensor("bp_s", [128, NDC], F32))
        bf_s = _ES.enter_context(nc.sbuf_tensor("bf_s", [128, NDC], F32))
        br_s = _ES.enter_context(nc.sbuf_tensor("br_s", [1, 8], F32))
        er8_s = _ES.enter_context(nc.sbuf_tensor("er8_s", [8, 128], BF16))
        md_s = _ES.enter_context(nc.sbuf_tensor("md_s", [128, 4, TW], BF16))
        ones_c = _ES.enter_context(nc.sbuf_tensor("ones_c", [128, 1], BF16))
        ones_r = _ES.enter_context(nc.sbuf_tensor("ones_r", [1, 128], BF16))
        ones_m = _ES.enter_context(nc.sbuf_tensor("ones_m", [128, 128], BF16))
        ones_rf = _ES.enter_context(nc.sbuf_tensor("ones_rf", [1, 128], F32))
        eps_s = _ES.enter_context(nc.sbuf_tensor("eps_s", [128, 1], F32))
        c256_s = _ES.enter_context(nc.sbuf_tensor("c256_s", [1, 1], F32))
        c64_s = _ES.enter_context(nc.sbuf_tensor("c64_s", [1, 1], F32))
        ballg = _ES.enter_context(nc.sbuf_tensor("ballg", [128, 2 * NDC], F32))
        qv_s = _ES.enter_context(nc.sbuf_tensor("qv_s", [1, TW], F32))
        mu1_s = _ES.enter_context(nc.sbuf_tensor("mu1_s", [1, TW], F32))
        rmk_s = _ES.enter_context(nc.sbuf_tensor("rmk_s", [1, TW], F32))
        wrt_s = _ES.enter_context(nc.sbuf_tensor("wrt_s", [128, NDC, 8], BF16))
        rqp_s = _ES.enter_context(nc.sbuf_tensor("rqp_s", [128, NDC, 4], F32))
        rqs_s = _ES.enter_context(nc.sbuf_tensor("rqs_s", [128, NDC], F32))
        rqb_s = _ES.enter_context(nc.sbuf_tensor("rqb_s", [128, NDC], BF16))
        kmx_b = _ES.enter_context(nc.sbuf_tensor("kmx_b", [128, 1], BF16))
        r8c_s = _ES.enter_context(nc.sbuf_tensor("r8c_s", [8, 1], BF16))
        rsc_s = _ES.enter_context(nc.sbuf_tensor("rsc_s", [128, 1], F32))
        t1_s = _ES.enter_context(nc.sbuf_tensor("t1_s", [1, TW], F32))
        t2_s = _ES.enter_context(nc.sbuf_tensor("t2_s", [1, TW], F32))
        t3_s = _ES.enter_context(nc.sbuf_tensor("t3_s", [1, TW], BF16))
        t4_s = _ES.enter_context(nc.sbuf_tensor("t4_s", [1, TW], F32))
        kn8_s = _ES.enter_context(nc.sbuf_tensor("kn8_s", [1, 8], F32))
        mx_s = _ES.enter_context(nc.sbuf_tensor("mx_s", [1, 2], F32))
        t8_s = _ES.enter_context(nc.sbuf_tensor("t8_s", [1, 8], F32))
        t8b_s = _ES.enter_context(nc.sbuf_tensor("t8b_s", [1, 1], F32))
        t8c_s = _ES.enter_context(nc.sbuf_tensor("t8c_s", [1, 8], BF16))
        P0 = _ES.enter_context(nc.psum_tensor([128, 512], F32))
        P1 = _ES.enter_context(nc.psum_tensor([128, 512], F32))
        P2 = _ES.enter_context(nc.psum_tensor([128, 512], F32))
        P3 = _ES.enter_context(nc.psum_tensor([128, 512], F32))
        P4 = _ES.enter_context(nc.psum_tensor([128, 512], F32))
        P5 = _ES.enter_context(nc.psum_tensor([128, 512], F32))
        P6 = _ES.enter_context(nc.psum_tensor([128, 512], F32))
        P7 = _ES.enter_context(nc.psum_tensor([128, 512], F32))
        block = _ES.enter_context(nc.Block())
        NAMES = ["iw", "ax0", "ax1", "aw0", "aw1", "as0", "as1", "am",
                 "bk", "bv", "bq0", "bq1", "bqv", "bo0", "bo1",
                 "co", "cw0", "cw1", "cm0", "cm1", "cx0", "cx1", "cxs",
                 "c30", "c31", "rsp", "rld",
                 "dab", "dx3", "dw0", "dw1", "d20", "d21", "do0", "do1", "ddb",
                 "ten_a", "vec_a", "sca_a", "ten_b", "vec_b", "sca_b",
                 "ten_c", "vec_c", "sca_c", "coll", "vbar", "sbar", "wcv"]
        SEM = {}
        for name in NAMES:
            SEM[name] = _ES.enter_context(nc.semaphore("s_" + name))
            val[name] = 0

        def inc(name, amt=1):
            val[name] += amt
            return val[name]

        def gd(name):
            return Gd(SEM[name], val[name])

        def W(e, g):
            if g is None:
                return
            if isinstance(g, list):
                for gg in g:
                    W(e, gg)
                return
            if g.sem is not None and g.val > 0:
                e.wait_ge(g.sem, g.val)

        def cf(off_b, n):
            assert off_b % 4 == 0
            return AR.ap()[:, off_b // 4: off_b // 4 + n]

        def cb(off_b, n):
            assert off_b % 4 == 0
            return AR.ap()[:, off_b // 4: off_b // 4 + (n + 1) // 2].bitcast(BF16)[:, 0:n]

        A_x1 = cb(0, NDC * 2048).rearrange("p (c t) -> p c t", c=NDC)
        A_xst = [cf(131072 + i * 4096, 1024).rearrange("p (a t) -> p a t", a=2) for i in range(2)]
        A_sq = [cb(139264 + i * 2048, 1024).rearrange("p (a t) -> p a t", a=2) for i in range(2)]
        A_qst = [cb(143360 + i * 4096, 4 * TW).rearrange("p (j t) -> p j t", j=4) for i in range(2)]
        A_wp = [cb(151552 + i * 8192, NDC * 128).rearrange("p (a c) -> p a c", a=NDC) for i in range(2)]
        A_f32s = cf(167936, 512)
        A_mu = cf(169984, 512)
        A_rs = cf(172032, 512)
        A_ms = cf(174080, 512)

        B_kT = [cb(0 + i * 8192, S) for i in range(2)]
        B_v = [cb(16384 + i * 8192, NDC * 128).rearrange("p (c d) -> p c d", c=NDC) for i in range(2)]
        B_qh = [cb(32768 + i * 8192, S) for i in range(2)]
        B_qs = cb(49152, S)
        B_pt = [cb(57344 + i * 1024, TW) for i in range(8)]
        B_sx = [cf(65536 + i * 2048, TW) for i in range(4)]
        B_rc = cf(114688, TW)
        B_oe = [cb(116736 + i * 1024, TW) for i in range(2)]
        B_qn = cf(118784, S)
        B_t3 = [cb(135168 + i * 8192, S) for i in range(2)]
        B_qvf = cf(151552, S)

        C_oT = cb(0, NDC * 1024).rearrange("p (c t) -> p c t", c=NDC)
        C_x2 = cf(0, NDC * 512).rearrange("p (c t) -> p c t", c=NDC)
        C_at = cb(65536, NDC * 1024).rearrange("p (c t) -> p c t", c=NDC)
        C_wp = [cb(131072 + i * 8192, NDC * 128).rearrange("p (a c) -> p a c", a=NDC) for i in range(2)]
        C_xr = [cf(147456 + i * 2048, TW) for i in range(2)]
        C_f1 = cf(155648, 512)
        C_mu = cf(157696, 512)
        C_rs = cf(159744, 512)
        C_ms = cf(161792, 512)
        C_sqs = [cb(163840 + i * 1024, TW) for i in range(2)]
        C_rmb = cf(166912, 512)
        C_x2b = cb(168960, TW)
        C_x4 = [cf(171008 + i * 2048, TW) for i in range(2)]

        D_x3 = cb(0, NDC * 1024).rearrange("p (c t) -> p c t", c=NDC)
        D_ff = cb(65536, NDC * 1024).rearrange("p (c t) -> p c t", c=NDC)
        D_wp = C_wp
        D_aT = cb(147456, NDC * 128).rearrange("p (a c) -> p a c", a=NDC)
        D_bB = cb(155648, NDC * 128).rearrange("p (o_c o) -> p o_c o", o_c=NDC)
        D_tt = [cb(163840 + i * 1024, TW) for i in range(2)]
        D_x2r = [cf(165888 + i * 2048, TW) for i in range(2)]
        D_f1 = cf(169984, 512)
        D_mu = cf(172032, 512)
        D_rs = cf(174080, 512)
        D_ms = cf(176128, 512)
        D_sqs = [cb(178176 + i * 1024, TW) for i in range(2)]
        D_x4 = [cf(180224 + i * 2048, TW) for i in range(2)]

        PS = [P0, P1, P2, P3, P4, P5, P6, P7]

        def init_gp(g):
            g.dma_start(kvec_s[:], kveci).then_inc(SEM["iw"], 16)
            g.dma_start(slp_s[:], slpi).then_inc(SEM["iw"], 16)
            g.dma_start(nslp_s[:], nslpi).then_inc(SEM["iw"], 16)
            g.dma_start(bqkv_s[:], bQKVpm).then_inc(SEM["iw"], 16)
            g.dma_start(bp_s[:], bPpm).then_inc(SEM["iw"], 16)
            g.dma_start(bf_s[:], bFpm).then_inc(SEM["iw"], 16)
            g.dma_start(br_s[:], bRi).then_inc(SEM["iw"], 16)
            g.dma_start(er8_s[:], er8i).then_inc(SEM["iw"], 16)
            g.dma_start(md_s[:, :, :], mdeli.rearrange("d p t -> p d t")).then_inc(SEM["iw"], 16)
            g.dma_start(wrt_s[:, :, :], wRT.rearrange("(a p) e -> p a e", p=128)).then_inc(SEM["iw"], 16)
            g.memset(ones_c[:], 1.0)
            g.memset(ones_r[:], 1.0)
            g.memset(ones_m[:], 1.0)
            g.memset(ones_rf[:], 1.0)
            g.memset(eps_s[:], EPS)
            g.memset(c256_s[:], 1.0 / 256.0)
            g.memset(c64_s[:], 64.0)
            g.memset(rqp_s[:, :, :], 0.0)
            g.fin(SEM["coll"], 1)
        em("gpsimd", init_gp)
        inc("iw", 10 * 16)
        inc("coll", 1)
        G_INIT = gd("coll")



        NG = 16
        RUN_A = ("A" in phases) or ("a" in phases)
        RUN_A2 = "A" in phases
        g_x1war = Gd()
        g_psAB = Gd()
        g_stage = [Gd(), Gd()]
        g_sq_cons = [Gd(), Gd()]
        g_x1tiles = []
        g_mu1st = Gd()
        g_wpan = [Gd(), Gd()]
        g_qst = [Gd(), Gd()]
        g_qkv_banks = [Gd(), Gd()]

        for sb in range(2 if RUN_A else 0):
            for tt in range(4):
                ti = sb * 4 + tt
                T0 = ti * TW
                sq_guards = []
                for gidx in range(NG):
                    buf = gidx % 2
                    gsnap = g_stage[buf].snap()
                    def a_load(sy, gidx=gidx, buf=buf, T0=T0, gsnap=gsnap, first=(ti == 0 and gidx == 0)):
                        W(sy, gsnap)
                        src = xT[:, T0:T0 + TW].rearrange("(a p) t -> p a t", p=128)[:, 2 * gidx:2 * gidx + 2, :]
                        sy.dma_start(A_xst[buf], src).then_inc(SEM["ax%d" % buf], 16)
                    em("sync", a_load)
                    inc("ax%d" % buf, 16)
                    g_ld = gd("ax%d" % buf)

                    gx1 = g_x1war.snap()
                    def a_copy(v, gidx=gidx, buf=buf, tt=tt, g_ld=g_ld, gx1=gx1):
                        W(v, g_ld)
                        W(v, gx1)
                        for k in range(2):
                            dc = 2 * gidx + k
                            v.tensor_copy(A_x1[:, dc, tt * TW:(tt + 1) * TW],
                                          A_xst[buf][:, k, :])
                        v.fin(SEM["vec_a"], 1)
                    em("vector", a_copy)
                    inc("vec_a", 1)
                    g_cp = gd("vec_a")

                    gsq = g_sq_cons[buf].snap()
                    def a_sqr(s, buf=buf, g_ld=g_ld, gsq=gsq):
                        W(s, g_ld)
                        W(s, gsq)
                        for k in range(2):
                            s.activation(A_sq[buf][:, k, :], A_xst[buf][:, k, :], AF.Square)
                        s.fin(SEM["sca_a"], 1)
                    em("scalar", a_sqr)
                    inc("sca_a", 1)
                    g_sqd = gd("sca_a")

                    gps = g_psAB.snap() if gidx == 0 else None
                    gini = G_INIT.snap() if (ti == 0 and gidx == 0) else None
                    def a_stat(t, gidx=gidx, buf=buf, tt=tt, g_cp=g_cp, g_sqd=g_sqd, gps=gps, gini=gini):
                        W(t, g_cp)
                        W(t, g_sqd)
                        if gini is not None:
                            W(t, gini)
                        if gps is not None:
                            W(t, gps)
                        for k in range(2):
                            dc = 2 * gidx + k
                            st = (gidx == 0 and k == 0)
                            sp = (gidx == NG - 1 and k == 1)
                            t.matmul(P0[:], ones_m[:], A_x1[:, dc, tt * TW:(tt + 1) * TW],
                                     start=st, stop=sp)
                            t.matmul(P1[:], ones_m[:], A_sq[buf][:, k, :],
                                     start=st, stop=sp)
                        t.fin(SEM["ten_a"], 1)
                    em("tensor", a_stat)
                    inc("ten_a", 1)
                    g_stage[buf] = Gd(SEM["ten_a"], val["ten_a"])
                    g_sq_cons[buf] = Gd(SEM["ten_a"], val["ten_a"])

                g_statall = gd("ten_a")
                def a_mid(v, g_statall=g_statall):
                    W(v, g_statall)
                    v.tensor_scalar(A_mu, P0[:], 1.0 / HID, None, ALU.mult)
                    v.tensor_scalar(A_ms, P1[:], 1.0 / HID, None, ALU.mult)
                    v.tensor_mul(A_rs, A_mu, A_mu)
                    v.tensor_sub(A_rs, A_ms, A_rs)
                    v.fin(SEM["vec_a"], 1)
                em("vector", a_mid)
                inc("vec_a", 1)
                g_psAB = gd("vec_a")
                g_mid = gd("vec_a")

                ginis = G_INIT.snap() if ti == 0 else None
                def a_rstd(s, g_mid=g_mid, ginis=ginis):
                    W(s, g_mid)
                    if ginis is not None:
                        W(s, ginis)
                    s.activation(A_rs, A_rs, AF.Ln, bias=eps_s[:, 0:1])
                    s.activation(A_rs, A_rs, AF.Exp, scale=-0.5)
                    s.fin(SEM["sca_a"], 1)
                em("scalar", a_rstd)
                inc("sca_a", 1)
                g_rstd = gd("sca_a")

                gm1 = g_mu1st.snap()
                def a_x1fin(v, tt=tt, g_rstd=g_rstd, gm1=gm1):
                    W(v, g_rstd)
                    W(v, gm1)
                    v.tensor_copy(t1_s[:], A_mu[0:1, :])
                    for dc in range(NDC):
                        xv = A_x1[:, dc, tt * TW:(tt + 1) * TW]
                        v.tensor_sub(A_f32s, xv, A_mu)
                        v.tensor_mul(xv, A_f32s, A_rs)
                    v.fin(SEM["vec_a"], 1)
                em("vector", a_x1fin)
                inc("vec_a", 1)
                g_x1tiles.append(gd("vec_a"))
                g_fin = gd("vec_a")

                def a_mu1st(sy, T0=T0, g_fin=g_fin):
                    W(sy, g_fin)
                    sy.dma_start(mu1sc[:, T0:T0 + TW], t1_s[:]).then_inc(SEM["am"], 16)
                em("sync", a_mu1st)
                inc("am", 16)
                g_mu1st = gd("am")

            for cc in range(NCC if RUN_A2 else 0):
                ci = sb * NCC + cc
                wbuf = ci % 2
                gw = g_wpan[wbuf].snap()
                def a_wld(g, cc=cc, wbuf=wbuf, gw=gw):
                    W(g, gw)
                    g.dma_start(A_wp[wbuf],
                                wQKV[cc].rearrange("(a p) c -> p a c", p=128)).then_inc(SEM["aw%d" % wbuf], 16)
                em("gpsimd", a_wld)
                inc("aw%d" % wbuf, 16)
                g_w = [gd("aw%d" % wbuf), gd("iw")]

                banks = [PS[2], PS[3], PS[4], PS[5]] if cc % 2 == 0 else [PS[0], PS[1], PS[6], PS[7]]
                gx1all = g_x1tiles[sb * 4 + 3].snap()
                gbk = g_qkv_banks[cc % 2].snap()
                def a_qkvmm(t, wbuf=wbuf, banks=banks, g_w=g_w, gx1all=gx1all, gbk=gbk):
                    W(t, g_w)
                    W(t, gx1all)
                    W(t, gbk)
                    for dc in range(NDC):
                        for j in range(4):
                            t.matmul(banks[j][:], A_wp[wbuf][:, dc, :],
                                     A_x1[:, dc, j * TW:(j + 1) * TW],
                                     start=(dc == 0), stop=(dc == NDC - 1))
                    t.fin(SEM["ten_a"], 1)
                em("tensor", a_qkvmm)
                inc("ten_a", 1)
                g_wpan[wbuf] = gd("ten_a")
                g_mmd = gd("ten_a")

                gst = g_qst[ci % 2].snap()
                def a_qkvev(v, cc=cc, ci=ci, banks=banks, g_mmd=g_mmd, gst=gst):
                    W(v, g_mmd)
                    W(v, gst)
                    isq = cc < 16
                    for j in range(4):
                        if isq:
                            v.tensor_scalar(A_qst[ci % 2][:, j, :], banks[j][:],
                                            bqkv_s[:, cc:cc + 1], 1.0 / 128.0, ALU.add, ALU.mult)
                        else:
                            v.tensor_scalar(A_qst[ci % 2][:, j, :], banks[j][:],
                                            bqkv_s[:, cc:cc + 1], None, ALU.add)
                    v.fin(SEM["vec_a"], 1)
                em("vector", a_qkvev)
                inc("vec_a", 1)
                g_qkv_banks[cc % 2] = gd("vec_a")
                g_ev = gd("vec_a")

                def a_qkvst(sy, cc=cc, ci=ci, sb=sb, g_ev=g_ev):
                    W(sy, g_ev)
                    sy.dma_start(qkvT[cc * 128:(cc + 1) * 128, sb * 2048:(sb + 1) * 2048]
                                 .rearrange("p (j t) -> p j t", j=4),
                                 A_qst[ci % 2]).then_inc(SEM["as%d" % (ci % 2)], 16)
                em("sync", a_qkvst)
                inc("as%d" % (ci % 2), 16)
                g_qst[ci % 2] = gd("as%d" % (ci % 2))

            g_x1war = gd("ten_a")
            g_psAB = gd("vec_a")

        G_AQKV = [gd("as0"), gd("as1"), gd("am")]

        def wconv(g):
            for i in range(8):
                g.dma_start(wPbf[i * 512:(i + 1) * 512, :],
                            wP[i * 512:(i + 1) * 512, :]).then_inc(SEM["wcv"], 16)
            for i in range(8):
                g.dma_start(wFbf.rearrange("c h w -> (c h) w")[i * 16384 // 128 * 32:(i + 1) * 4096, :] if False else
                            wFbf[i * 4:(i + 1) * 4],
                            wFT[i * 4:(i + 1) * 4]).then_inc(SEM["wcv"], 16)
        em("gpsimd", wconv)
        inc("wcv", 16 * 16)
        G_WCV = gd("wcv")

        RUN_B = "B" in phases
        RUN_C = "C" in phases
        RUN_D = "D" in phases
        g_kbuf = Gd()
        g_qbuf = [Gd(), Gd()]
        g_sxbuf = [Gd() for _ in range(8)]
        g_ptbuf = [Gd() for _ in range(8)]
        g_scoreps = [Gd(), Gd()]
        g_p2 = Gd()
        g_p3 = Gd()
        g_p4 = Gd()
        g_p5 = Gd()
        g_oebuf = [Gd(), Gd()]
        g_kn = Gd()

        SKW = 3
        g_po = [Gd(), Gd()]
        g_pd = [Gd(), Gd()]
        PO = [P2, P6]
        PD = [P3, P7]
        pending_epi = [None]
        pend_pv = []
        cglob = [0]
        g_bqs = Gd()
        g_p45 = [Gd(), Gd()]
        PP45 = [P4, P5]
        g_mx = [Gd(), Gd()]
        g_ballg = [Gd(), Gd()]
        g_t3 = [Gd(), Gd()]
        g_ncg = [[Gd() for _ in range(8)], [Gd() for _ in range(8)]]
        headinfo = {}

        def emit_kvload(h):
            par = h % 2
            gkv = [gd("ten_b"), gd("sca_b"), gd("vec_b")]
            gaq = [g.snap() for g in G_AQKV] if h == 0 else None
            def b_kld(sy, h=h, par=par, gkv=gkv, gaq=gaq):
                if gaq is not None:
                    W(sy, gaq)
                W(sy, gkv)
                sy.dma_start(B_kT[par], qkvT[2048 + h * 128: 2048 + (h + 1) * 128, :]).then_inc(SEM["bk"], 16)
                sy.dma_start(B_qh[par], qkvT[h * 128:(h + 1) * 128, :]).then_inc(SEM["bq%d" % par], 16)
            em("sync", b_kld)
            inc("bk", 16)
            inc("bq%d" % par, 16)
            gk = gd("bk")
            gq = gd("bq%d" % par)
            def b_vld(sy, h=h, par=par, gkv=gkv, gaq=gaq):
                if gaq is not None:
                    W(sy, gaq)
                W(sy, gkv)
                sy.dma_start(B_v[par],
                             qkvT[4096 + h * 128: 4096 + (h + 1) * 128, :]
                             .rearrange("p (c q) -> p c q", c=NDC),
                             transpose=True).then_inc(SEM["bv"], 16)
            em("sync", b_vld)
            inc("bv", 16)
            return gk, gq, gd("bv")

        def emit_headprep(h):
            nonlocal g_bqs
            par = h % 2
            g_kd, g_qld, g_vdx = headinfo[h]["loads"]

            def b_kmx(v, par=par, g_kd=g_kd):
                W(v, g_kd)
                v.reduce_max(kmx_b[:], B_kT[par], AXX, apply_absolute_value=True)
                v.bar()
                v.tensor_mul(kmx_b[:], kmx_b[:], kmx_b[:])
                v.fin(SEM["vec_b"], 1)
            em("vector", b_kmx)
            inc("vec_b", 1)
            g_kmx = gd("vec_b")

            gp = g_p45[0].snap()
            def b_kn2(t, g_kmx=g_kmx, gp=gp):
                W(t, g_kmx)
                W(t, gp)
                t.matmul(P4[0:1, 0:1], ones_c[:], kmx_b[:], start=True, stop=True).then_inc(SEM["ten_b"], 1)
            em("tensor", b_kn2)
            inc("ten_b", 1)
            g_kn2 = gd("ten_b")

            gmx = g_mx[par].snap()
            def b_mx(v, par=par, g_kn2=g_kn2, gmx=gmx):
                W(v, g_kn2)
                W(v, gmx)
                v.tensor_scalar(mx_s[0:1, par:par + 1], P4[0:1, 0:1], c256_s[0:1, 0:1], 1.0,
                                ALU.mult, ALU.add)
                v.fin(SEM["vec_b"], 1)
            em("vector", b_mx)
            inc("vec_b", 1)
            g_mxs = gd("vec_b")
            g_p45[0] = gd("vec_b")

            gbl = g_ballg[par].snap()
            def b_ball(s, h=h, par=par, g_kd=g_kd, gbl=gbl):
                W(s, g_kd)
                W(s, gbl)
                s.mul(ballg[:, par * NDC:(par + 1) * NDC], kvec_s[:], slp_s[:, h:h + 1])
                s.fin(SEM["sca_b"], 1)
            em("scalar", b_ball)
            inc("sca_b", 1)
            g_balld = gd("sca_b")

            gbq = g_bqs.snap()
            def b_qsq(s, par=par, g_qld=g_qld, gbq=gbq):
                W(s, g_qld)
                W(s, gbq)
                s.activation(B_qs, B_qh[par], AF.Square)
                s.fin(SEM["sca_b"], 1)
            em("scalar", b_qsq)
            inc("sca_b", 1)
            g_qsqd = gd("sca_b")

            for i in range(8):
                gp = g_p45[i % 2].snap()
                def b_qn2(t, i=i, par=par, g_qsqd=g_qsqd, gp=gp):
                    if i == 0:
                        W(t, g_qsqd)
                    W(t, gp)
                    t.matmul(PP45[i % 2][0:1, :], ones_c[:], B_qs[:, i * 512:(i + 1) * 512],
                             start=True, stop=True).then_inc(SEM["ten_b"], 1)
                em("tensor", b_qn2)
                inc("ten_b", 1)
                g_q2 = gd("ten_b")

                def b_qnc(v, i=i, par=par, g_q2=g_q2):
                    W(v, g_q2)
                    v.tensor_copy(B_qn[0:1, i * 512:(i + 1) * 512], PP45[i % 2][0:1, :])
                    v.fin(SEM["vec_b"], 1)
                em("vector", b_qnc)
                inc("vec_b", 1)
                g_p45[i % 2] = gd("vec_b")
            g_bqs = gd("ten_b")
            g_qnd = gd("vec_b")

            gt3 = g_t3[par].snap()
            def b_crow(v, h=h, par=par, g_qnd=g_qnd, g_mxs=g_mxs, gt3=gt3):
                W(v, g_mxs)
                W(v, gt3)
                v.tensor_scalar(B_qn[0:1, :], B_qn[0:1, :], c64_s[0:1, 0:1], mx_s[0:1, par:par + 1],
                                ALU.mult, ALU.add)
                v.bar()
                v.scalar_tensor_tensor(B_t3[par][0:1, :], B_qvf[0:1, :], nslp_s[0:1, h:h + 1], B_qn[0:1, :],
                                       ALU.mult, ALU.subtract)
                v.fin(SEM["vec_b"], 1)
            em("vector", b_crow)
            inc("vec_b", 1)
            headinfo[h]["prep"] = gd("vec_b")
            headinfo[h]["ball"] = g_balld

        def emit_epilogue_mk(hl, qt, qp, qb, g_pvall):
            def doit():
                def b_recip(s, qp=qp, g_pvall=g_pvall):
                    W(s, g_pvall)
                    s.activation(t4_s[:], PD[qp][0:1, :], AF.Ln)
                    s.activation(t4_s[:], t4_s[:], AF.Exp, scale=-1.0)
                    s.fin(SEM["sca_b"], 1)
                em("scalar", b_recip)
                inc("sca_b", 1)
                g_rec = gd("sca_b")
                g_pd[qp].sem, g_pd[qp].val = SEM["sca_b"], val["sca_b"]

                def b_r2b(v, g_rec=g_rec):
                    W(v, g_rec)
                    v.tensor_copy(t3_s[:], t4_s[:])
                    v.fin(SEM["vec_b"], 1)
                em("vector", b_r2b)
                inc("vec_b", 1)
                g_r2b = gd("vec_b")

                gp4b = g_p45[0].snap()
                def b_rbc(t, g_r2b=g_r2b, gp4b=gp4b):
                    W(t, g_r2b)
                    W(t, gp4b)
                    t.matmul(P4[:], ones_r[:], t3_s[:], start=True, stop=True).then_inc(SEM["ten_b"], 1)
                em("tensor", b_rbc)
                inc("ten_b", 1)
                g_rbc = gd("ten_b")

                goe = g_oebuf[qb].snap()
                def b_oev(v, qp=qp, qb=qb, g_rbc=g_rbc, goe=goe):
                    W(v, g_rbc)
                    W(v, goe)
                    v.tensor_copy(B_rc, P4[:])
                    v.bar()
                    v.tensor_mul(B_oe[qb], PO[qp][:], B_rc)
                    v.fin(SEM["vec_b"], 1)
                em("vector", b_oev)
                inc("vec_b", 1)
                g_p45[0].sem, g_p45[0].val = SEM["vec_b"], val["vec_b"]
                g_po[qp].sem, g_po[qp].val = SEM["vec_b"], val["vec_b"]
                g_oed = gd("vec_b")

                def b_ost(sy, hl=hl, qt=qt, qb=qb, g_oed=g_oed):
                    W(sy, g_oed)
                    sy.dma_start(ag_in.ap()[hl * 128:(hl + 1) * 128, qt * TW:(qt + 1) * TW],
                                 B_oe[qb]).then_inc(SEM["bo%d" % qb], 16)
                em("sync", b_ost)
                inc("bo%d" % qb, 16)
                g_oebuf[qb].sem, g_oebuf[qb].val = SEM["bo%d" % qb], val["bo%d" % qb]
            return doit

        def flush_pv(n):
            while len(pend_pv) > n:
                pend_pv.pop(0)()

        if RUN_B:
            def b_qvf(sy, gaq=[g.snap() for g in G_AQKV]):
                sy.dma_start(B_qvf[0:1, :], qveci).then_inc(SEM["bqv"], 16)
            em("sync", b_qvf)
            inc("bqv", 16)
            g_qvf = gd("bqv")
            g_ncg = [[Gd(sem=None) for _ in range(8)] for _ in range(2)]
            g_ncg = [[Gd() for _ in range(8)], [Gd() for _ in range(8)]]
            headinfo[0] = {"loads": emit_kvload(0)}
            g_t3[0] = g_qvf
            g_t3[1] = g_qvf
            emit_headprep(0)

        for hl in range(NHC if RUN_B else 0):
            par = hl % 2
            g_kd, g_qld, g_vd = headinfo[hl]["loads"]
            g_prep = headinfo[hl]["prep"]
            g_balld = headinfo[hl]["ball"]

            for qt in range(8):
                nch = 4 * (qt + 1)
                qb = qt % 2
                qp = qt % 2
                exp_g = {}
                done_next = [False]

                for k in range(nch):
                    cg = cglob[0] % 8
                    cglob[0] += 1
                    cb2 = k % 2
                    dd = k - (nch - 4)
                    masked = dd >= 0
                    sxi = k % 4
                    gsc = g_scoreps[cb2].snap()
                    def b_mms(t, hl=hl, par=par, qt=qt, k=k, cb2=cb2, gsc=gsc,
                              gq=(g_qld.snap() if (qt == 0 and k == 0) else None),
                              gpp=(g_prep.snap() if (qt == 0 and k == 0) else None)):
                        W(t, gsc)
                        if gq is not None:
                            W(t, gq)
                        if gpp is not None:
                            W(t, gpp)
                        t.matmul(PS[cb2][:], B_kT[par][:, k * 128:(k + 1) * 128],
                                 B_qh[par][:, qt * TW:(qt + 1) * TW],
                                 start=True, stop=False)
                        t.matmul(PS[cb2][:], ones_r[:], B_t3[par][0:1, qt * TW:(qt + 1) * TW],
                                 start=False, stop=True).then_inc(SEM["ten_b"], 1)
                    em("tensor", b_mms)
                    inc("ten_b", 1)
                    g_sd = gd("ten_b")

                    col = par * NDC + k
                    if masked:
                        gsx = g_sxbuf[sxi].snap()
                        def b_msk(v, cb2=cb2, sxi=sxi, dd=dd, g_sd=g_sd, gsx=gsx):
                            W(v, g_sd)
                            W(v, gsx)
                            v.tensor_add(B_sx[sxi], PS[cb2][:], md_s[:, dd, :])
                            v.fin(SEM["vec_b"], 1)
                        em("vector", b_msk)
                        inc("vec_b", 1)
                        g_mskd = gd("vec_b")
                        g_scoreps[cb2] = g_mskd

                    gpt = g_ptbuf[cg].snap()
                    if masked:
                        def b_exp(s, cg=cg, sxi=sxi, col=col, g_mskd=g_mskd, gpt=gpt):
                            W(s, g_mskd)
                            W(s, gpt)
                            s.activation(B_pt[cg], B_sx[sxi], AF.Exp, bias=ballg[:, col:col + 1])
                            s.fin(SEM["sca_b"], 1)
                    else:
                        def b_exp(s, cg=cg, cb2=cb2, col=col, g_sd=g_sd, gpt=gpt):
                            W(s, g_sd)
                            W(s, gpt)
                            s.activation(B_pt[cg], PS[cb2][:], AF.Exp, bias=ballg[:, col:col + 1])
                            s.fin(SEM["sca_b"], 1)
                    em("scalar", b_exp)
                    inc("sca_b", 1)
                    if masked:
                        g_sxbuf[sxi] = gd("sca_b")
                    else:
                        g_scoreps[cb2] = gd("sca_b")
                    exp_g[k] = gd("sca_b")

                    def mk_pv(par=par, qp=qp, j=k, cg=cg, g_expd=exp_g[k],
                              g_vdx=(g_vd.snap() if (qt == 0 and k == 0) else None),
                              last=(k == nch - 1), hl=hl, qt=qt, qb=qb):
                        def do_pv():
                            if j == 0 and pending_epi[0] is not None:
                                pending_epi[0]()
                                pending_epi[0] = None
                            gpox = g_po[qp].snap() if j == 0 else None
                            gpdx = g_pd[qp].snap() if j == 0 else None
                            def b_pv(t, par=par, qp=qp, j=j, cg=cg, g_expd=g_expd, g_vdx=g_vdx,
                                     gpox=gpox, gpdx=gpdx, last=last):
                                W(t, g_expd)
                                if g_vdx is not None:
                                    W(t, g_vdx)
                                if gpox is not None:
                                    W(t, gpox)
                                    W(t, gpdx)
                                t.matmul(PO[qp][:], B_v[par][:, j, :], B_pt[cg],
                                         start=(j == 0), stop=last)
                                t.matmul(PD[qp][0:1, :], ones_c[:], B_pt[cg],
                                         start=(j == 0), stop=last).then_inc(SEM["ten_b"], 1)
                            em("tensor", b_pv)
                            inc("ten_b", 1)
                            g_ptbuf[cg].sem, g_ptbuf[cg].val = SEM["ten_b"], val["ten_b"]
                            if last:
                                assert pending_epi[0] is None
                                pending_epi[0] = emit_epilogue_mk(hl, qt, qp, qb, gd("ten_b"))
                        return do_pv
                    pend_pv.append(mk_pv())
                    flush_pv(SKW)

                    if qt == 2 and k == 0 and hl + 1 < NHC and (hl + 1) not in headinfo:
                        headinfo[hl + 1] = {"loads": emit_kvload(hl + 1)}
                    if qt == 4 and k == 0 and hl + 1 < NHC and "prep" not in headinfo.get(hl + 1, {}):
                        emit_headprep(hl + 1)
            g_t3[par] = gd("ten_b")

        if RUN_B:
            flush_pv(0)
            if pending_epi[0] is not None:
                pending_epi[0]()
                pending_epi[0] = None

        G_BOST = [gd("bo0"), gd("bo1")]

        if not RUN_B:
            RUN_C = False
        def b_ag(g, G_BOST=[g.snap() for g in G_BOST]):
            W(g, G_BOST)
            g.collective_compute("AllGather", ALU.bypass,
                                 replica_groups=[[0, 1], [2, 3], [4, 5], [6, 7]],
                                 ins=[ag_in.ap().opt()], outs=[ag_out.ap().opt()]).then_inc(SEM["coll"], 1)
        if RUN_C:
            em("gpsimd", b_ag)
            inc("coll", 1)
        G_AG = gd("coll")

        g_oThalf = Gd()
        g_wpanC = [Gd(), Gd()]
        g_projps = [Gd(), Gd()]
        g_p45C = Gd()
        g_p6C = Gd()
        g_xrbuf = [Gd(), Gd()]
        g_sqsbuf = [Gd(), Gd()]
        g_x2tile = Gd()
        g_at_s = Gd()

        for hf in range(2 if RUN_C else 0):
            goh = [g.snap() for g in g_oThalf] if isinstance(g_oThalf, list) else g_oThalf.snap()
            gag = G_AG.snap()
            def c_old(sy, hf=hf, goh=goh, gag=gag):
                W(sy, gag)
                W(sy, goh)
                for arm in sy.switch_core_id(n=128):
                    g_id = int(arm.logical) % 2
                    base = g_id * 2048 + hf * 1024
                    sy.dma_start(C_oT, ag_out.ap()[:, base:base + 1024]
                                 .rearrange("(c p) t -> p c t", p=128)).then_inc(SEM["co"], 16)
            em("sync", c_old)
            inc("co", 16)
            g_ohd = gd("co")

            for oc in range(NDC):
                oi = hf * NDC + oc
                gwp = g_wpanC[oc % 2].snap()
                def c_wld(g, oc=oc, gwp=gwp, gwc=(G_WCV.snap() if oi <= 1 else None)):
                    W(g, gwp)
                    if gwc is not None:
                        W(g, gwc)
                    g.dma_start(C_wp[oc % 2], wPbf[:, oc * 128:(oc + 1) * 128]
                                .rearrange("(a p) c -> p a c", p=128)).then_inc(SEM["cw%d" % (oc % 2)], 16)
                em("gpsimd", c_wld)
                inc("cw%d" % (oc % 2), 16)
                g_wd = [gd("cw%d" % (oc % 2)), gd("iw")]

                banks = [PS[0], PS[1]] if oc % 2 == 0 else [PS[2], PS[3]]
                gpp = g_projps[oc % 2].snap()
                def c_mm(t, oc=oc, banks=banks, g_wd=g_wd, g_ohd=g_ohd, gpp=gpp):
                    W(t, g_wd)
                    W(t, g_ohd)
                    W(t, gpp)
                    for dc in range(NDC):
                        for j in range(2):
                            t.matmul(banks[j][:], C_wp[oc % 2][:, dc, :],
                                     C_oT[:, dc, j * TW:(j + 1) * TW],
                                     start=(dc == 0), stop=(dc == NDC - 1))
                    t.fin(SEM["ten_c"], 1)
                em("tensor", c_mm)
                inc("ten_c", 1)
                g_wpanC[oc % 2] = gd("ten_c")
                g_mmd = gd("ten_c")

                gat = g_at_s.snap() if (oc == 0) else None
                def c_ev(v, oc=oc, banks=banks, g_mmd=g_mmd, gat=gat):
                    W(v, g_mmd)
                    if gat is not None:
                        W(v, gat)
                    for j in range(2):
                        v.tensor_scalar(C_at[:, oc, j * TW:(j + 1) * TW], banks[j][:],
                                        bp_s[:, oc:oc + 1], None, ALU.add)
                    v.fin(SEM["vec_c"], 1)
                em("vector", c_ev)
                inc("vec_c", 1)
                g_projps[oc % 2] = gd("vec_c")

            g_evall = gd("vec_c")

            for lt in range(2):
                tglob = hf * 2 + lt
                gp45 = g_p45C.snap()
                sq_guard_list = []
                for dc in range(NDC):
                    gsq = g_sqsbuf[dc % 2].snap()
                    def c_sq(s, dc=dc, lt=lt, g_evall=g_evall, gsq=gsq):
                        if dc == 0:
                            W(s, g_evall)
                        W(s, gsq)
                        s.activation(C_sqs[dc % 2], C_at[:, dc, lt * TW:(lt + 1) * TW],
                                     AF.Square)
                        s.fin(SEM["sca_c"], 1)
                    em("scalar", c_sq)
                    inc("sca_c", 1)
                    g_sqd = gd("sca_c")

                    def c_st(t, dc=dc, lt=lt, g_sqd=g_sqd, gp45=(gp45 if dc == 0 else None)):
                        W(t, g_sqd)
                        if gp45 is not None:
                            W(t, gp45)
                        t.matmul(P4[:], ones_m[:], C_at[:, dc, lt * TW:(lt + 1) * TW],
                                 start=(dc == 0), stop=(dc == NDC - 1))
                        t.matmul(P5[:], ones_m[:], C_sqs[dc % 2],
                                 start=(dc == 0), stop=(dc == NDC - 1)).then_inc(SEM["ten_c"], 1)
                    em("tensor", c_st)
                    inc("ten_c", 1)
                    g_sqsbuf[dc % 2] = gd("ten_c")

                g_st2 = gd("ten_c")
                def c_mid(v, g_st2=g_st2):
                    W(v, g_st2)
                    v.tensor_scalar(C_mu, P4[:], 1.0 / HID, None, ALU.mult)
                    v.tensor_scalar(C_ms, P5[:], 1.0 / HID, None, ALU.mult)
                    v.tensor_mul(C_rs, C_mu, C_mu)
                    v.tensor_sub(C_rs, C_ms, C_rs)
                    v.fin(SEM["vec_c"], 1)
                em("vector", c_mid)
                inc("vec_c", 1)
                g_p45C = gd("vec_c")
                g_mid2 = gd("vec_c")

                def c_rstd(s, g_mid2=g_mid2):
                    W(s, g_mid2)
                    s.activation(C_rs, C_rs, AF.Ln, bias=eps_s[:, 0:1])
                    s.activation(C_rs, C_rs, AF.Exp, scale=-0.5)
                    s.fin(SEM["sca_c"], 1)
                em("scalar", c_rstd)
                inc("sca_c", 1)
                g_rstd2 = gd("sca_c")

                def c_m1ld(sy, tglob=tglob, g_rstd2=g_rstd2):
                    W(sy, g_rstd2)
                    for arm in sy.switch_core_id(n=128):
                        g_id = int(arm.logical) % 2
                        gbase = g_id * 2048 + tglob * TW
                        sy.dma_start(mu1_s[:], mu1sc[:, gbase:gbase + TW]).then_inc(SEM["cm%d" % (tglob % 2)], 16)
                    sy.dma_start(rmk_s[:], rmaski[:, tglob * TW:(tglob + 1) * TW]).then_inc(SEM["cm%d" % (tglob % 2)], 16)
                em("sync", c_m1ld)
                inc("cm%d" % (tglob % 2), 32)
                g_m1d = gd("cm%d" % (tglob % 2))

                gp6 = g_p6C.snap()
                def c_m1b(t, g_m1d=g_m1d, gp6=gp6):
                    W(t, g_m1d)
                    W(t, gp6)
                    t.matmul(P6[:], ones_rf[:], mu1_s[:], start=True, stop=True).then_inc(SEM["ten_c"], 1)
                em("tensor", c_m1b)
                inc("ten_c", 1)
                g_m1b = gd("ten_c")

                def c_m1c(v, g_m1b=g_m1b):
                    W(v, g_m1b)
                    v.tensor_copy(C_ms, P6[:])
                    v.fin(SEM["vec_c"], 1)
                em("vector", c_m1c)
                inc("vec_c", 1)
                g_m1c = gd("vec_c")

                def c_rmb(t, g_m1c=g_m1c):
                    W(t, g_m1c)
                    t.matmul(P6[:], ones_rf[:], rmk_s[:], start=True, stop=True).then_inc(SEM["ten_c"], 1)
                em("tensor", c_rmb)
                inc("ten_c", 1)
                g_rmbd = gd("ten_c")

                def c_rmc(v, g_rmbd=g_rmbd):
                    W(v, g_rmbd)
                    v.tensor_copy(C_rmb, P6[:])
                    v.fin(SEM["vec_c"], 1)
                em("vector", c_rmc)
                inc("vec_c", 1)
                g_p6C = gd("vec_c")

                gx2t = g_x2tile.snap()
                for dc in range(NDC):
                    xb = dc % 2
                    gxr = g_xrbuf[xb].snap()
                    def c_xld(sy, dc=dc, xb=xb, tglob=tglob, gxr=gxr):
                        W(sy, gxr)
                        sy.dma_start(C_xr[xb], xTown[dc * 128:(dc + 1) * 128,
                                                     tglob * TW:(tglob + 1) * TW]).then_inc(SEM["cx%d" % xb], 16)
                    em("sync", c_xld)
                    inc("cx%d" % xb, 16)
                    g_xd = gd("cx%d" % xb)

                    def c_x2(v, dc=dc, xb=xb, lt=lt, g_xd=g_xd, gx2t=(gx2t if dc == 0 else None)):
                        W(v, g_xd)
                        if gx2t is not None:
                            W(v, gx2t)
                        v.tensor_sub(C_f1, C_at[:, dc, lt * TW:(lt + 1) * TW], C_mu)
                        v.tensor_mul(C_f1, C_f1, C_rs)
                        v.tensor_add(C_x2[:, dc, :], C_f1, C_xr[xb])
                        v.fin(SEM["vec_c"], 1)
                    em("vector", c_x2)
                    inc("vec_c", 1)
                    g_xrbuf[xb] = gd("vec_c")
                    g_x2d = gd("vec_c")

                    def c_x2sp(sy, dc=dc, tglob=tglob, g_x2d=g_x2d):
                        W(sy, g_x2d)
                        sy.dma_start(x2sc[dc * 128:(dc + 1) * 128, tglob * TW:(tglob + 1) * TW],
                                     C_x2[:, dc, :]).then_inc(SEM["cxs"], 16)
                    em("sync", c_x2sp)
                    inc("cxs", 16)
                g_x2tile = gd("cxs")

                gp45b = g_p45C.snap()
                for dc in range(NDC):
                    gsq2 = g_sqsbuf[0].snap()
                    def c_x2b(v, dc=dc, gsq2=gsq2):
                        W(v, gsq2)
                        v.tensor_copy(C_x2b, C_x2[:, dc, :])
                        v.fin(SEM["vec_c"], 1)
                    em("vector", c_x2b)
                    inc("vec_c", 1)
                    g_cb = gd("vec_c")

                    def c_sq3(s, g_cb=g_cb):
                        W(s, g_cb)
                        s.activation(C_sqs[0], C_x2b, AF.Square)
                        s.fin(SEM["sca_c"], 1)
                    em("scalar", c_sq3)
                    inc("sca_c", 1)
                    g_sq3 = gd("sca_c")

                    def c_st3(t, dc=dc, g_sq3=g_sq3, gp45b=(gp45b if dc == 0 else None)):
                        W(t, g_sq3)
                        if gp45b is not None:
                            W(t, gp45b)
                        t.matmul(P5[:], ones_m[:], C_sqs[0],
                                 start=(dc == 0), stop=(dc == NDC - 1)).then_inc(SEM["ten_c"], 1)
                    em("tensor", c_st3)
                    inc("ten_c", 1)
                    g_sqsbuf[0] = gd("ten_c")

                g_st3 = gd("ten_c")
                def c_mid3(v, g_st3=g_st3):
                    W(v, g_st3)
                    v.tensor_scalar(C_f1, P5[:], 1.0 / HID, None, ALU.mult)
                    v.tensor_mul(C_rs, C_ms, C_ms)
                    v.tensor_sub(C_rs, C_f1, C_rs)
                    v.fin(SEM["vec_c"], 1)
                em("vector", c_mid3)
                inc("vec_c", 1)
                g_p45C = gd("vec_c")
                g_mid3 = gd("vec_c")

                def c_rstd3(s, g_mid3=g_mid3):
                    W(s, g_mid3)
                    s.activation(C_rs, C_rs, AF.Ln, bias=eps_s[:, 0:1])
                    s.activation(C_rs, C_rs, AF.Exp, scale=-0.5)
                    s.fin(SEM["sca_c"], 1)
                em("scalar", c_rstd3)
                inc("sca_c", 1)
                g_rstd3 = gd("sca_c")

                for dc in range(NDC):
                    gsq3 = g_sqsbuf[dc % 2].snap()
                    def c_x3(v, dc=dc, tglob=tglob, g_rstd3=g_rstd3, gsq3=gsq3,
                             GI=(G_INIT.snap() if tglob == 0 and dc == 0 else None)):
                        if GI is not None:
                            W(v, GI)
                        if dc == 0:
                            W(v, g_rstd3)
                        W(v, gsq3)
                        v.tensor_sub(C_f1, C_x2[:, dc, :], C_ms)
                        v.tensor_mul(C_sqs[dc % 2], C_f1, C_rs)
                        v.tensor_mul(C_f1, C_sqs[dc % 2], C_rmb)
                        v.reduce_sum(rqp_s[:, dc, tglob:tglob + 1], C_f1, AXX)
                        v.fin(SEM["vec_c"], 1)
                    em("vector", c_x3)
                    inc("vec_c", 1)
                    g_x3d = gd("vec_c")

                    def c_x3sp(sy, dc=dc, tglob=tglob, g_x3d=g_x3d):
                        W(sy, g_x3d)
                        sy.dma_start(x3sc[dc * 128:(dc + 1) * 128, tglob * TW:(tglob + 1) * TW],
                                     C_sqs[dc % 2]).then_inc(SEM["c3%d" % (dc % 2)], 16)
                    em("sync", c_x3sp)
                    inc("c3%d" % (dc % 2), 16)
                    g_sqsbuf[dc % 2] = gd("c3%d" % (dc % 2))

            g_oThalf = [gd("c30"), gd("c31")]
            g_at_s = gd("sca_c")
        G_CX3 = [gd("c30"), gd("c31"), gd("cxs")]

        if not RUN_C:
            RUN_D = False
        RTR = RUN_C
        g_vlast = gd("vec_c")
        def r_sum(v, g_vlast=g_vlast, GI=G_INIT.snap()):
            W(v, GI)
            v.tensor_add(rqs_s[:], rqp_s[:, :, 0], rqp_s[:, :, 1])
            v.tensor_add(rqs_s[:], rqs_s[:], rqp_s[:, :, 2])
            v.tensor_add(rqs_s[:], rqs_s[:], rqp_s[:, :, 3])
            v.fin(SEM["vec_c"], 1)
        if RTR:
            em("vector", r_sum)
            inc("vec_c", 1)
        g_rsum = gd("vec_c")

        def r_spill(sy, g_rsum=g_rsum):
            W(sy, g_rsum)
            sy.dma_start(rq_in.ap(), rqs_s[:]).then_inc(SEM["rsp"], 16)
        if RTR:
            em("sync", r_spill)
            inc("rsp", 16)
        g_rsp = gd("rsp")

        def r_ar(g, g_rsp=g_rsp):
            W(g, g_rsp)
            g.collective_compute("AllReduce", ALU.add,
                                 replica_groups=[[0, 1], [2, 3], [4, 5], [6, 7]],
                                 ins=[rq_in.ap().opt()], outs=[rq_out.ap().opt()]).then_inc(SEM["coll"], 1)
        if RTR:
            em("gpsimd", r_ar)
            inc("coll", 1)
        g_rar = gd("coll")

        def r_load(sy, g_rar=g_rar):
            W(sy, g_rar)
            sy.dma_start(rqs_s[:], rq_out.ap()).then_inc(SEM["rld"], 16)
        if RTR:
            em("sync", r_load)
            inc("rld", 16)
        g_rld = gd("rld")

        def r_b16(v, g_rld=g_rld):
            W(v, g_rld)
            v.tensor_scalar(rqb_s[:], rqs_s[:], 1.0 / 3484.0, None, ALU.mult)
            v.fin(SEM["vec_c"], 1)
        if RTR:
            em("vector", r_b16)
            inc("vec_c", 1)
        g_rb16 = gd("vec_c")

        gp6r = g_p6C.snap()
        def r_mm(t, g_rb16=g_rb16, gp6r=gp6r):
            W(t, g_rb16)
            W(t, gp6r)
            for dc in range(NDC):
                t.matmul(P6[0:1, 0:8], rqb_s[:, dc:dc + 1], wrt_s[:, dc, :],
                         start=(dc == 0), stop=(dc == NDC - 1))
            t.fin(SEM["ten_c"], 1)
        if RTR:
            em("tensor", r_mm)
            inc("ten_c", 1)
        g_rmm = gd("ten_c")

        def r_soft(v, g_rmm=g_rmm):
            W(v, g_rmm)
            v.tensor_add(t8_s[:], P6[0:1, 0:8], br_s[:])
            v.bar()
            v.reduce_max(t8b_s[:], t8_s[:], AXX)
            v.bar()
            v.tensor_scalar(t8_s[:], t8_s[:], t8b_s[0:1, 0:1], None, ALU.subtract)
            v.fin(SEM["vec_c"], 1)
        if RTR:
            em("vector", r_soft)
            inc("vec_c", 1)
        g_rsf = gd("vec_c")

        def r_exp(s, g_rsf=g_rsf):
            W(s, g_rsf)
            s.activation(t8_s[:], t8_s[:], AF.Exp)
            s.fin(SEM["sca_c"], 1)
        if RTR:
            em("scalar", r_exp)
            inc("sca_c", 1)
        g_rexp = gd("sca_c")

        def r_norm(v, g_rexp=g_rexp):
            W(v, g_rexp)
            v.reduce_sum(t8b_s[:], t8_s[:], AXX)
            v.bar()
            v.reciprocal(t8b_s[:], t8b_s[:])
            v.bar()
            v.tensor_scalar(t8_s[:], t8_s[:], t8b_s[0:1, 0:1], 2.0, ALU.mult, ALU.mult)
            v.bar()
            v.tensor_copy(t8c_s[:], t8_s[:])
            v.fin(SEM["vec_c"], 1)
        if RTR:
            em("vector", r_norm)
            inc("vec_c", 1)
        g_rnm = gd("vec_c")

        _rb_target = val["rsp"] + 16
        def r_bounce(sy, g_rnm=g_rnm, _rb_target=_rb_target):
            W(sy, g_rnm)
            sy.dma_start(r8dram, t8c_s[:]).then_inc(SEM["rsp"], 16)
            sy.wait_ge(SEM["rsp"], _rb_target)
            sy.dma_start(r8c_s[:], r8dram.rearrange("one e -> e one")).then_inc(SEM["rld"], 16)
        if RTR:
            em("sync", r_bounce)
            inc("rsp", 16)
            inc("rld", 16)
        g_rbn = gd("rld")

        def r_ermm(t, g_rbn=g_rbn):
            W(t, g_rbn)
            t.matmul(P6[:, 0:1], er8_s[:], r8c_s[:], start=True, stop=True).then_inc(SEM["ten_c"], 1)
        if RTR:
            em("tensor", r_ermm)
            inc("ten_c", 1)
        g_rem = gd("ten_c")

        def r_rsc(v, g_rem=g_rem):
            W(v, g_rem)
            v.tensor_copy(rsc_s[:], P6[:, 0:1])
            v.fin(SEM["vec_c"], 1)
        if RTR:
            em("vector", r_rsc)
            inc("vec_c", 1)
        G_RSC = gd("vec_c")
        g_p6C = gd("vec_c")

        g_x3half = Gd()
        g_wpanD = [Gd(), Gd()]
        g_ffps = [Gd(), Gd()]
        g_p45D = Gd()
        g_ttps = Gd()
        g_ttbuf = [Gd(), Gd()]
        g_x2rbuf = [Gd(), Gd()]
        g_x4buf = [Gd(), Gd()]
        g_sqsD = [Gd(), Gd()]
        g_ffree = Gd()

        def d_abld(g, G_CX3=[g.snap() for g in G_CX3]):
            W(g, G_CX3)
            g.dma_start(D_aT, aTi.rearrange("(a p) c -> p a c", p=128)).then_inc(SEM["dab"], 16)
            g.dma_start(D_bB, bBi.rearrange("p (c o) -> p c o", c=NDC)).then_inc(SEM["dab"], 16)
        if RUN_D:
            em("gpsimd", d_abld)
            inc("dab", 32)
        g_abd = gd("dab")

        for hf in range(2 if RUN_D else 0):
            gx3h = g_x3half.snap()
            gcx3 = [g.snap() for g in G_CX3] if hf == 0 else None
            def d_x3ld(sy, hf=hf, gx3h=gx3h, gcx3=gcx3):
                if gcx3 is not None:
                    W(sy, gcx3)
                W(sy, gx3h)
                sy.dma_start(D_x3, x3sc[:, hf * 1024:(hf + 1) * 1024]
                             .rearrange("(c p) t -> p c t", p=128)).then_inc(SEM["dx3"], 16)
            em("sync", d_x3ld)
            inc("dx3", 16)
            g_x3d = gd("dx3")

            for j in range(2):
                gtt = g_ttbuf[j].snap()
                gt7 = g_ttps.snap()
                def d_la(t, j=j, g_x3d=g_x3d, g_abd=g_abd, gt7=gt7):
                    W(t, g_x3d)
                    W(t, g_abd)
                    W(t, gt7)
                    for dc in range(NDC):
                        t.matmul(P7[:], D_aT[:, dc, :], D_x3[:, dc, j * TW:(j + 1) * TW],
                                 start=(dc == 0), stop=(dc == NDC - 1))
                    t.fin(SEM["ten_c"], 1)
                em("tensor", d_la)
                inc("ten_c", 1)
                g_lad = gd("ten_c")

                def d_lev(v, j=j, g_lad=g_lad, gtt=gtt, G_RSC=G_RSC.snap()):
                    W(v, g_lad)
                    W(v, gtt)
                    W(v, G_RSC)
                    v.tensor_scalar(D_tt[j], P7[:], rsc_s[:, 0:1], None, ALU.mult)
                    v.fin(SEM["vec_c"], 1)
                em("vector", d_lev)
                inc("vec_c", 1)
                g_ttps = gd("vec_c")
                g_ttbuf[j] = Gd()

            g_ttd = gd("vec_c")

            for oc in range(NDC):
                gwp = g_wpanD[oc % 2].snap()
                def d_wld(g, hf=hf, oc=oc, gwp=gwp, gwc=(G_WCV.snap() if (hf == 0 and oc <= 1) else None)):
                    W(g, gwp)
                    if gwc is not None:
                        W(g, gwc)
                    g.dma_start(D_wp[oc % 2], wFbf[oc].rearrange("(a p) c -> p a c", p=128)).then_inc(SEM["dw%d" % (oc % 2)], 16)
                em("gpsimd", d_wld)
                inc("dw%d" % (oc % 2), 16)
                g_wd = [gd("dw%d" % (oc % 2)), gd("iw")]

                banks = [PS[0], PS[1]] if oc % 2 == 0 else [PS[2], PS[3]]
                gpf = g_ffps[oc % 2].snap()
                def d_mm(t, oc=oc, banks=banks, g_wd=g_wd, g_ttd=g_ttd, gpf=gpf):
                    W(t, g_wd)
                    W(t, g_ttd)
                    W(t, gpf)
                    for dc in range(NDC):
                        for j in range(2):
                            t.matmul(banks[j][:], D_wp[oc % 2][:, dc, :],
                                     D_x3[:, dc, j * TW:(j + 1) * TW],
                                     start=(dc == 0), stop=False)
                    for j in range(2):
                        t.matmul(banks[j][:], D_bB[:, oc, :], D_tt[j],
                                 start=False, stop=True)
                    t.fin(SEM["ten_c"], 1)
                em("tensor", d_mm)
                inc("ten_c", 1)
                g_wpanD[oc % 2] = gd("ten_c")
                g_mmd = gd("ten_c")

                gff = g_ffree.snap() if (hf == 1 and oc == 0) else None
                def d_ev(v, oc=oc, banks=banks, g_mmd=g_mmd, gff=gff):
                    W(v, g_mmd)
                    if gff is not None:
                        W(v, gff)
                    for j in range(2):
                        v.tensor_scalar(D_ff[:, oc, j * TW:(j + 1) * TW], banks[j][:],
                                        bf_s[:, oc:oc + 1], None, ALU.add)
                    v.fin(SEM["vec_c"], 1)
                em("vector", d_ev)
                inc("vec_c", 1)
                g_ffps[oc % 2] = gd("vec_c")

            g_evall = gd("vec_c")
            g_ttbuf[0] = gd("ten_c")
            g_ttbuf[1] = gd("ten_c")

            for lt in range(2):
                tglob = hf * 2 + lt
                gp45 = g_p45D.snap()
                for dc in range(NDC):
                    gsq = g_sqsD[dc % 2].snap()
                    def d_sq(s, dc=dc, lt=lt, g_evall=g_evall, gsq=gsq):
                        if dc == 0:
                            W(s, g_evall)
                        W(s, gsq)
                        s.activation(D_sqs[dc % 2], D_ff[:, dc, lt * TW:(lt + 1) * TW], AF.Square)
                        s.fin(SEM["sca_c"], 1)
                    em("scalar", d_sq)
                    inc("sca_c", 1)
                    g_sqd = gd("sca_c")

                    def d_st(t, dc=dc, lt=lt, g_sqd=g_sqd, gp45=(gp45 if dc == 0 else None)):
                        W(t, g_sqd)
                        if gp45 is not None:
                            W(t, gp45)
                        t.matmul(P4[:], ones_m[:], D_ff[:, dc, lt * TW:(lt + 1) * TW],
                                 start=(dc == 0), stop=(dc == NDC - 1))
                        t.matmul(P5[:], ones_m[:], D_sqs[dc % 2],
                                 start=(dc == 0), stop=(dc == NDC - 1)).then_inc(SEM["ten_c"], 1)
                    em("tensor", d_st)
                    inc("ten_c", 1)
                    g_sqsD[dc % 2] = gd("ten_c")

                g_st4 = gd("ten_c")
                def d_mid(v, g_st4=g_st4):
                    W(v, g_st4)
                    v.tensor_scalar(D_mu, P4[:], 1.0 / HID, None, ALU.mult)
                    v.tensor_scalar(D_ms, P5[:], 1.0 / HID, None, ALU.mult)
                    v.tensor_mul(D_rs, D_mu, D_mu)
                    v.tensor_sub(D_rs, D_ms, D_rs)
                    v.fin(SEM["vec_c"], 1)
                em("vector", d_mid)
                inc("vec_c", 1)
                g_p45D = gd("vec_c")
                g_mid4 = gd("vec_c")

                def d_rstd(s, g_mid4=g_mid4):
                    W(s, g_mid4)
                    s.activation(D_rs, D_rs, AF.Ln, bias=eps_s[:, 0:1])
                    s.activation(D_rs, D_rs, AF.Exp, scale=-0.5)
                    s.fin(SEM["sca_c"], 1)
                em("scalar", d_rstd)
                inc("sca_c", 1)
                g_rstd4 = gd("sca_c")

                for dc in range(NDC):
                    xb = dc % 2
                    gx2 = g_x2rbuf[xb].snap()
                    def d_x2ld(sy, dc=dc, xb=xb, tglob=tglob, gx2=gx2):
                        W(sy, gx2)
                        sy.dma_start(D_x2r[xb], x2sc[dc * 128:(dc + 1) * 128,
                                                     tglob * TW:(tglob + 1) * TW]).then_inc(SEM["d2%d" % xb], 16)
                    em("sync", d_x2ld)
                    inc("d2%d" % xb, 16)
                    g_x2d = gd("d2%d" % xb)

                    gx4 = g_x4buf[xb].snap()
                    def d_x4(v, dc=dc, xb=xb, lt=lt, g_x2d=g_x2d, g_rstd4=g_rstd4, gx4=gx4):
                        if dc == 0:
                            W(v, g_rstd4)
                        W(v, g_x2d)
                        W(v, gx4)
                        v.tensor_sub(D_f1, D_ff[:, dc, lt * TW:(lt + 1) * TW], D_mu)
                        v.tensor_mul(D_f1, D_f1, D_rs)
                        v.tensor_add(D_x4[xb], D_f1, D_x2r[xb])
                        v.fin(SEM["vec_c"], 1)
                    em("vector", d_x4)
                    inc("vec_c", 1)
                    g_x2rbuf[xb] = gd("vec_c")
                    g_x4d = gd("vec_c")

                    def d_ost(sy, dc=dc, xb=xb, tglob=tglob, g_x4d=g_x4d):
                        W(sy, g_x4d)
                        sy.dma_start(outT[dc * 128:(dc + 1) * 128, tglob * TW:(tglob + 1) * TW],
                                     D_x4[xb]).then_inc(SEM["do%d" % xb], 16)
                    em("sync", d_ost)
                    inc("do%d" % xb, 16)
                    g_x4buf[xb] = gd("do%d" % xb)

            g_x3half = gd("ten_c")
            g_ffree = gd("vec_c")

        if dbg:
            G_ALL = [gd(n) for n in ["do0", "do1", "as0", "as1", "bo0", "bo1",
                                     "cxs", "c30", "c31", "am"]]
            def d_dbg(g, G_ALL=[g.snap() for g in G_ALL]):
                W(g, G_ALL)
                g.dma_start(dbg_outs["dqkv"], qkvT).then_inc(SEM["ddb"], 16)
                if RUN_C:
                    g.dma_start(dbg_outs["dag"], ag_out.ap()).then_inc(SEM["ddb"], 16)
                    g.dma_start(dbg_outs["dx2"], x2sc).then_inc(SEM["ddb"], 16)
                    g.dma_start(dbg_outs["dx3"], x3sc).then_inc(SEM["ddb"], 16)
                    g.dma_start(dbg_outs["drt"], rsc_s[:]).then_inc(SEM["ddb"], 16)
            em("gpsimd", d_dbg)
            inc("ddb", 80 if RUN_C else 16)

        G_FIN = [gd("do0"), gd("do1"), gd("ddb")]
        def fin(sy, G_FIN=[g.snap() for g in G_FIN]):
            W(sy, G_FIN)
        em("sync", fin)

        @block.sync
        def _(sy):
            p = _EngProxy(sy)
            for fn in Q["sync"]:
                fn(p)

        @block.gpsimd
        def _(g):
            p = _EngProxy(g)
            for fn in Q["gpsimd"]:
                fn(p)

        @block.tensor
        def _(t):
            p = _EngProxy(t)
            for fn in Q["tensor"]:
                fn(p)

        @block.vector
        def _(v):
            p = _EngProxy(v, SEM["vbar"])
            for fn in Q["vector"]:
                fn(p)

        @block.scalar
        def _(s):
            p = _EngProxy(s, SEM["sbar"])
            for fn in Q["scalar"]:
                fn(p)

    return nc


def _alibi_slopes32():
    import math
    return np.array([2.0 ** (-(i + 1) / 4.0) for i in range(32)], dtype=np.float32)


def _prep_core_inputs(c, x, w_attn, b_attn, w_proj, b_proj, W, bff, A, B, rW, rb):
    b, g = c // 2, c % 2
    hset = _hset_host(g)
    xTb = np.ascontiguousarray(x[b].T)
    xTown = np.ascontiguousarray(xTb[:, g * 2048:(g + 1) * 2048])

    qcols = np.concatenate([np.arange(h * 128, (h + 1) * 128) for h in hset])
    cols = np.concatenate([qcols, 4096 + qcols, 8192 + qcols])
    wq = np.ascontiguousarray(w_attn[:, cols])
    wq = np.ascontiguousarray(wq.reshape(4096, 48, 128).transpose(1, 0, 2))
    bq = b_attn[cols]
    bqpm = np.ascontiguousarray(bq.reshape(48, 128).T)

    perm = np.concatenate([np.arange(h * 128, (h + 1) * 128)
                           for h in _hset_host(0) + _hset_host(1)])
    wp = np.ascontiguousarray(w_proj[perm, :])
    bppm = np.ascontiguousarray(b_proj.reshape(32, 128).T)

    wft = np.ascontiguousarray(W.T.reshape(4096, 32, 128).transpose(1, 0, 2))
    bfpm = np.ascontiguousarray(bff.reshape(32, 128).T)

    aT = np.ascontiguousarray(A.transpose(2, 0, 1).reshape(4096, 128))
    bB = np.ascontiguousarray(B.transpose(0, 2, 1).reshape(128, 4096))
    wrt = np.ascontiguousarray(rW.T)
    brr = np.ascontiguousarray(rb.reshape(1, 8))

    kvec = (np.arange(32)[None, :] * 128 + np.arange(128)[:, None]).astype(np.float32)
    qvec = np.arange(4096, dtype=np.float32).reshape(1, 4096)
    sl = _alibi_slopes32()
    slp = np.ascontiguousarray(np.broadcast_to(sl[hset][None, :], (128, 16))).astype(np.float32)
    nslp = (-sl[hset]).reshape(1, 16).astype(np.float32)
    ii, jj = np.meshgrid(np.arange(128), np.arange(512), indexing="ij")
    mdel = np.zeros((4, 128, 512), dtype=np.float32)
    for dd in range(4):
        mdel[dd][(ii + 128 * dd) > jj] = -1e30
    er8 = np.zeros((8, 128), dtype=np.float32)
    for e in range(8):
        er8[e, e * 16:(e + 1) * 16] = 1.0
    rmask = np.zeros((1, 4096), dtype=np.float32)
    rmask[0, 611:4095] = 1.0
    rmask = np.ascontiguousarray(rmask[:, g * 2048:(g + 1) * 2048])

    return {
        "xT": xTb, "xTown": xTown, "wQKV": wq, "bQKVpm": bqpm,
        "wP": wp, "bPpm": bppm, "wFT": wft, "bFpm": bfpm,
        "aTi": aT, "bBi": bB, "wRT": wrt, "bRi": brr,
        "kveci": kvec, "qveci": qvec, "slpi": slp, "nslpi": nslp,
        "mdeli": mdel, "er8i": er8, "rmaski": rmask,
    }


def _hset_host(g):
    return list(range(0, 8)) + list(range(16, 24)) if g == 0 else \
           list(range(8, 16)) + list(range(24, 32))


def kernel(L0_attn_hidden_states, L0_attn_w_attn, L0_attn_b_attn, L0_attn_w_proj,
           L0_attn_b_proj, L0_ffn_W, L0_ffn_b, L0_ffn_A, L0_ffn_B,
           L0_ffn_router_W, L0_ffn_router_b):
    x = np.asarray(L0_attn_hidden_states, dtype=np.float32)
    args = (x, np.asarray(L0_attn_w_attn, np.float32), np.asarray(L0_attn_b_attn, np.float32),
            np.asarray(L0_attn_w_proj, np.float32), np.asarray(L0_attn_b_proj, np.float32),
            np.asarray(L0_ffn_W, np.float32), np.asarray(L0_ffn_b, np.float32),
            np.asarray(L0_ffn_A, np.float32), np.asarray(L0_ffn_B, np.float32),
            np.asarray(L0_ffn_router_W, np.float32), np.asarray(L0_ffn_router_b, np.float32))
    nc = build_program(dbg=0)
    in_maps = [_prep_core_inputs(c, *args) for c in range(8)]
    res = run_bass_kernel_spmd(nc, in_maps, core_ids=list(range(8)))
    out = np.empty((4, 4096, 4096), dtype=np.float32)
    for c in range(8):
        b, g = c // 2, c % 2
        out[b, g * 2048:(g + 1) * 2048, :] = res.results[c]["outT"].T
    return out


# version 09
# speedup vs baseline: 1.3196x; 1.3196x over previous
import numpy as np
import concourse.bass as bass
import concourse.mybir as mybir
from concourse.bass_utils import run_bass_kernel_spmd

F32 = mybir.dt.float32
BF16 = mybir.dt.bfloat16
AF = mybir.ActivationFunctionType
ALU = mybir.AluOpType
AXX = mybir.AxisListType.X

HID = 4096
S = 4096
NHC = 16
NCC = 48
NDC = 32
TW = 512
EPS = 1e-5
NOWN = 2048


class Gd:
    __slots__ = ("sem", "val")

    def __init__(self, sem=None, val=0):
        self.sem = sem
        self.val = val

    def snap(self):
        return Gd(self.sem, self.val)


class _EngProxy:
    _SKIP = {"wait_ge", "nop", "maybe_wait_ge", "sem_inc", "br", "dma_start",
             "dma_start_transpose", "collective_compute", "switch_core_id"}

    def __init__(self, e, barsem=None):
        object.__setattr__(self, "_e", e)
        object.__setattr__(self, "last", None)
        object.__setattr__(self, "barsem", barsem)
        object.__setattr__(self, "barcount", 0)

    def __getattr__(self, k):
        a = getattr(object.__getattribute__(self, "_e"), k)
        if callable(a) and k not in _EngProxy._SKIP and not k.startswith("_"):
            def wrap(*args, **kw):
                r = a(*args, **kw)
                if hasattr(r, "then_inc"):
                    object.__setattr__(self, "last", r)
                return r
            return wrap
        return a

    def bar(self):
        last = object.__getattribute__(self, "last")
        sem = object.__getattribute__(self, "barsem")
        assert last is not None and sem is not None
        cnt = object.__getattribute__(self, "barcount") + 1
        object.__setattr__(self, "barcount", cnt)
        last.then_inc(sem, 1)
        object.__setattr__(self, "last", None)
        object.__getattribute__(self, "_e").wait_ge(sem, cnt)

    def fin(self, sem, n=1):
        last = object.__getattribute__(self, "last")
        assert last is not None, "fin() with no preceding datapath op"
        last.then_inc(sem, n)
        object.__setattr__(self, "last", None)


def build_program(dbg=0, phases="ABCD"):
    nc = bass.Bass("TRN2", target_bir_lowering=False)

    xT = nc.dram_tensor("xT", [HID, S], F32, kind="ExternalInput").ap()
    xTown = nc.dram_tensor("xTown", [HID, NOWN], F32, kind="ExternalInput").ap()
    wQKV = nc.dram_tensor("wQKV", [NCC, HID, 128], F32, kind="ExternalInput").ap()
    bQKVpm = nc.dram_tensor("bQKVpm", [128, NCC], F32, kind="ExternalInput").ap()
    wP = nc.dram_tensor("wP", [HID, HID], F32, kind="ExternalInput").ap()
    bPpm = nc.dram_tensor("bPpm", [128, NDC], F32, kind="ExternalInput").ap()
    wFT = nc.dram_tensor("wFT", [NDC, HID, 128], F32, kind="ExternalInput").ap()
    bFpm = nc.dram_tensor("bFpm", [128, NDC], F32, kind="ExternalInput").ap()
    aTi = nc.dram_tensor("aTi", [HID, 128], F32, kind="ExternalInput").ap()
    bBi = nc.dram_tensor("bBi", [128, HID], F32, kind="ExternalInput").ap()
    wRT = nc.dram_tensor("wRT", [HID, 8], F32, kind="ExternalInput").ap()
    bRi = nc.dram_tensor("bRi", [1, 8], F32, kind="ExternalInput").ap()
    kveci = nc.dram_tensor("kveci", [128, NDC], F32, kind="ExternalInput").ap()
    qveci = nc.dram_tensor("qveci", [1, S], F32, kind="ExternalInput").ap()
    slpi = nc.dram_tensor("slpi", [128, NHC], F32, kind="ExternalInput").ap()
    nslpi = nc.dram_tensor("nslpi", [1, NHC], F32, kind="ExternalInput").ap()
    mdeli = nc.dram_tensor("mdeli", [4, 128, TW], F32, kind="ExternalInput").ap()
    er8i = nc.dram_tensor("er8i", [8, 128], F32, kind="ExternalInput").ap()
    rmaski = nc.dram_tensor("rmaski", [1, NOWN], F32, kind="ExternalInput").ap()

    outT = nc.dram_tensor("outT", [HID, NOWN], F32, kind="ExternalOutput").ap()

    qkvT = nc.dram_tensor("qkvT", [3 * 2048, S], BF16, kind="Internal").ap()
    ag_in = nc.dram_tensor("ag_in", [2048, S], BF16, kind="Internal")
    ag_out = nc.dram_tensor("ag_out", [HID, S], BF16, kind="Internal")
    rq_in = nc.dram_tensor("rq_in", [128, NDC], F32, kind="Internal")
    rq_out = nc.dram_tensor("rq_out", [128, NDC], F32, kind="Internal")
    r8dram = nc.dram_tensor("r8dram", [1, 8], BF16, kind="Internal").ap()
    x2sc = nc.dram_tensor("x2sc", [HID, NOWN], F32, kind="Internal").ap()
    wPbf = nc.dram_tensor("wPbf", [HID, HID], BF16, kind="Internal").ap()
    wFbf = nc.dram_tensor("wFbf", [NDC, HID, 128], BF16, kind="Internal").ap()
    x3sc = nc.dram_tensor("x3sc", [HID, NOWN], BF16, kind="Internal").ap()
    mu1sc = nc.dram_tensor("mu1sc", [1, S], F32, kind="Internal").ap()

    dbg_outs = {}
    if dbg:
        dbg_outs["dqkv"] = nc.dram_tensor("dqkv", [3 * 2048, S], BF16, kind="ExternalOutput").ap()
        dbg_outs["dag"] = nc.dram_tensor("dag", [HID, S], BF16, kind="ExternalOutput").ap()
        dbg_outs["dx2"] = nc.dram_tensor("dx2", [HID, NOWN], F32, kind="ExternalOutput").ap()
        dbg_outs["dx3"] = nc.dram_tensor("dx3", [HID, NOWN], BF16, kind="ExternalOutput").ap()
        dbg_outs["drt"] = nc.dram_tensor("drt", [128, 1], F32, kind="ExternalOutput").ap()

    Q = {k: [] for k in ("sync", "gpsimd", "tensor", "vector", "scalar")}

    def em(eng, fn):
        Q[eng].append(fn)

    sems = {}
    val = {}

    import contextlib
    with contextlib.ExitStack() as _ES:
        AR = _ES.enter_context(nc.sbuf_tensor("AR", [128, 46080], F32))
        kvec_s = _ES.enter_context(nc.sbuf_tensor("kvec_s", [128, NDC], F32))
        slp_s = _ES.enter_context(nc.sbuf_tensor("slp_s", [128, NHC], F32))
        nslp_s = _ES.enter_context(nc.sbuf_tensor("nslp_s", [1, NHC], F32))
        bqkv_s = _ES.enter_context(nc.sbuf_tensor("bqkv_s", [128, NCC], F32))
        bp_s = _ES.enter_context(nc.sbuf_tensor("bp_s", [128, NDC], F32))
        bf_s = _ES.enter_context(nc.sbuf_tensor("bf_s", [128, NDC], F32))
        br_s = _ES.enter_context(nc.sbuf_tensor("br_s", [1, 8], F32))
        er8_s = _ES.enter_context(nc.sbuf_tensor("er8_s", [8, 128], BF16))
        md_s = _ES.enter_context(nc.sbuf_tensor("md_s", [128, 4, TW], BF16))
        ones_c = _ES.enter_context(nc.sbuf_tensor("ones_c", [128, 1], BF16))
        ones_r = _ES.enter_context(nc.sbuf_tensor("ones_r", [1, 128], BF16))
        ones_m = _ES.enter_context(nc.sbuf_tensor("ones_m", [128, 128], BF16))
        ones_rf = _ES.enter_context(nc.sbuf_tensor("ones_rf", [1, 128], F32))
        eps_s = _ES.enter_context(nc.sbuf_tensor("eps_s", [128, 1], F32))
        c256_s = _ES.enter_context(nc.sbuf_tensor("c256_s", [1, 1], F32))
        c64_s = _ES.enter_context(nc.sbuf_tensor("c64_s", [1, 1], F32))
        ballg = _ES.enter_context(nc.sbuf_tensor("ballg", [128, 2 * NDC], F32))
        qv_s = _ES.enter_context(nc.sbuf_tensor("qv_s", [1, TW], F32))
        mu1_s = _ES.enter_context(nc.sbuf_tensor("mu1_s", [1, TW], F32))
        rmk_s = _ES.enter_context(nc.sbuf_tensor("rmk_s", [1, TW], F32))
        wrt_s = _ES.enter_context(nc.sbuf_tensor("wrt_s", [128, NDC, 8], BF16))
        rqp_s = _ES.enter_context(nc.sbuf_tensor("rqp_s", [128, NDC, 4], F32))
        rqs_s = _ES.enter_context(nc.sbuf_tensor("rqs_s", [128, NDC], F32))
        rqb_s = _ES.enter_context(nc.sbuf_tensor("rqb_s", [128, NDC], BF16))
        kmx_b = _ES.enter_context(nc.sbuf_tensor("kmx_b", [128, 1], BF16))
        r8c_s = _ES.enter_context(nc.sbuf_tensor("r8c_s", [8, 1], BF16))
        rsc_s = _ES.enter_context(nc.sbuf_tensor("rsc_s", [128, 1], F32))
        t1_s = _ES.enter_context(nc.sbuf_tensor("t1_s", [1, TW], F32))
        t2_s = _ES.enter_context(nc.sbuf_tensor("t2_s", [1, TW], F32))
        t3_s = _ES.enter_context(nc.sbuf_tensor("t3_s", [1, TW], BF16))
        t4_s = _ES.enter_context(nc.sbuf_tensor("t4_s", [1, TW], F32))
        kn8_s = _ES.enter_context(nc.sbuf_tensor("kn8_s", [1, 8], F32))
        mx_s = _ES.enter_context(nc.sbuf_tensor("mx_s", [1, 2], F32))
        t8_s = _ES.enter_context(nc.sbuf_tensor("t8_s", [1, 8], F32))
        t8b_s = _ES.enter_context(nc.sbuf_tensor("t8b_s", [1, 1], F32))
        t8c_s = _ES.enter_context(nc.sbuf_tensor("t8c_s", [1, 8], BF16))
        P0 = _ES.enter_context(nc.psum_tensor([128, 512], F32))
        P1 = _ES.enter_context(nc.psum_tensor([128, 512], F32))
        P2 = _ES.enter_context(nc.psum_tensor([128, 512], F32))
        P3 = _ES.enter_context(nc.psum_tensor([128, 512], F32))
        P4 = _ES.enter_context(nc.psum_tensor([128, 512], F32))
        P5 = _ES.enter_context(nc.psum_tensor([128, 512], F32))
        P6 = _ES.enter_context(nc.psum_tensor([128, 512], F32))
        P7 = _ES.enter_context(nc.psum_tensor([128, 512], F32))
        block = _ES.enter_context(nc.Block())
        NAMES = ["iw", "ax0", "ax1", "aw0", "aw1", "as0", "as1", "am",
                 "bk", "bv", "bq0", "bq1", "bqv", "bo0", "bo1",
                 "co", "cw0", "cw1", "cm0", "cm1", "cx0", "cx1", "cxs",
                 "c30", "c31", "rsp", "rld",
                 "dab", "dx3", "dw0", "dw1", "d20", "d21", "do0", "do1", "ddb",
                 "ten_a", "vec_a", "sca_a", "ten_b", "vec_b", "sca_b",
                 "ten_c", "vec_c", "sca_c", "coll", "vbar", "sbar", "wcv", "btz"]
        SEM = {}
        for name in NAMES:
            SEM[name] = _ES.enter_context(nc.semaphore("s_" + name))
            val[name] = 0

        def inc(name, amt=1):
            val[name] += amt
            return val[name]

        def gd(name):
            return Gd(SEM[name], val[name])

        def W(e, g):
            if g is None:
                return
            if isinstance(g, list):
                for gg in g:
                    W(e, gg)
                return
            if g.sem is not None and g.val > 0:
                e.wait_ge(g.sem, g.val)

        def cf(off_b, n):
            assert off_b % 4 == 0
            return AR.ap()[:, off_b // 4: off_b // 4 + n]

        def cb(off_b, n):
            assert off_b % 4 == 0
            return AR.ap()[:, off_b // 4: off_b // 4 + (n + 1) // 2].bitcast(BF16)[:, 0:n]

        A_x1 = cb(0, NDC * 2048).rearrange("p (c t) -> p c t", c=NDC)
        A_xst = [cf(131072 + i * 4096, 1024).rearrange("p (a t) -> p a t", a=2) for i in range(2)]
        A_sq = [cb(139264 + i * 2048, 1024).rearrange("p (a t) -> p a t", a=2) for i in range(2)]
        A_qst = [cb(143360 + i * 4096, 4 * TW).rearrange("p (j t) -> p j t", j=4) for i in range(2)]
        A_wp = [cb(151552 + i * 8192, NDC * 128).rearrange("p (a c) -> p a c", a=NDC) for i in range(2)]
        A_f32s = cf(167936, 512)
        A_mu = cf(169984, 512)
        A_rs = cf(172032, 512)
        A_ms = cf(174080, 512)

        B_kT = [cb(0 + i * 8192, S) for i in range(2)]
        B_v = [cb(16384 + i * 8192, NDC * 128).rearrange("p (c d) -> p c d", c=NDC) for i in range(2)]
        B_qh = [cb(32768 + i * 8192, S) for i in range(2)]
        B_qs = cb(49152, S)
        B_pt = [cb(57344 + i * 1024, TW) for i in range(8)]
        B_sx = [cf(65536 + i * 2048, TW) for i in range(4)]
        B_rc = cf(114688, TW)
        B_oe = [cb(116736 + i * 1024, TW) for i in range(2)]
        B_qn = cf(118784, S)
        B_t3 = [cb(81920 + i * 8192, S) for i in range(2)]
        B_t3z = cb(81920, 2 * S)
        B_qvf = cf(98304, S)

        C_oT = cb(0, NDC * 1024).rearrange("p (c t) -> p c t", c=NDC)
        C_x2 = cf(0, NDC * 512).rearrange("p (c t) -> p c t", c=NDC)
        C_at = cb(65536, NDC * 1024).rearrange("p (c t) -> p c t", c=NDC)
        C_wp = [cb(131072 + i * 8192, NDC * 128).rearrange("p (a c) -> p a c", a=NDC) for i in range(2)]
        C_xr = [cf(147456 + i * 2048, TW) for i in range(2)]
        C_f1 = cf(155648, 512)
        C_mu = cf(157696, 512)
        C_rs = cf(159744, 512)
        C_ms = cf(161792, 512)
        C_sqs = [cb(163840 + i * 1024, TW) for i in range(2)]
        C_rmb = cf(166912, 512)
        C_x2b = cb(168960, TW)
        C_x4 = [cf(171008 + i * 2048, TW) for i in range(2)]

        D_x3 = cb(0, NDC * 1024).rearrange("p (c t) -> p c t", c=NDC)
        D_ff = cb(65536, NDC * 1024).rearrange("p (c t) -> p c t", c=NDC)
        D_wp = C_wp
        D_aT = cb(147456, NDC * 128).rearrange("p (a c) -> p a c", a=NDC)
        D_bB = cb(155648, NDC * 128).rearrange("p (o_c o) -> p o_c o", o_c=NDC)
        D_tt = [cb(163840 + i * 1024, TW) for i in range(2)]
        D_x2r = [cf(165888 + i * 2048, TW) for i in range(2)]
        D_f1 = cf(169984, 512)
        D_mu = cf(172032, 512)
        D_rs = cf(174080, 512)
        D_ms = cf(176128, 512)
        D_sqs = [cb(178176 + i * 1024, TW) for i in range(2)]
        D_x4 = [cf(180224 + i * 2048, TW) for i in range(2)]

        PS = [P0, P1, P2, P3, P4, P5, P6, P7]

        def init_gp(g):
            g.dma_start(kvec_s[:], kveci).then_inc(SEM["iw"], 16)
            g.dma_start(slp_s[:], slpi).then_inc(SEM["iw"], 16)
            g.dma_start(nslp_s[:], nslpi).then_inc(SEM["iw"], 16)
            g.dma_start(bqkv_s[:], bQKVpm).then_inc(SEM["iw"], 16)
            g.dma_start(bp_s[:], bPpm).then_inc(SEM["iw"], 16)
            g.dma_start(bf_s[:], bFpm).then_inc(SEM["iw"], 16)
            g.dma_start(br_s[:], bRi).then_inc(SEM["iw"], 16)
            g.dma_start(er8_s[:], er8i).then_inc(SEM["iw"], 16)
            g.dma_start(md_s[:, :, :], mdeli.rearrange("d p t -> p d t")).then_inc(SEM["iw"], 16)
            g.dma_start(wrt_s[:, :, :], wRT.rearrange("(a p) e -> p a e", p=128)).then_inc(SEM["iw"], 16)
            g.memset(ones_c[:], 1.0)
            g.memset(ones_r[:], 1.0)
            g.memset(ones_m[:], 1.0)
            g.memset(ones_rf[:], 1.0)
            g.memset(eps_s[:], EPS)
            g.memset(c256_s[:], 1.0 / 256.0)
            g.memset(c64_s[:], 64.0)
            g.memset(rqp_s[:, :, :], 0.0)
            g.fin(SEM["coll"], 1)
        em("gpsimd", init_gp)
        inc("iw", 10 * 16)
        inc("coll", 1)
        G_INIT = gd("coll")



        NG = 16
        RUN_A = ("A" in phases) or ("a" in phases)
        RUN_A2 = "A" in phases
        g_x1war = Gd()
        g_psAB = Gd()
        g_stage = [Gd(), Gd()]
        g_sq_cons = [Gd(), Gd()]
        g_x1tiles = []
        g_mu1st = Gd()
        g_wpan = [Gd(), Gd()]
        g_qst = [Gd(), Gd()]
        g_qkv_banks = [Gd(), Gd()]

        for sb in range(2 if RUN_A else 0):
            for tt in range(4):
                ti = sb * 4 + tt
                T0 = ti * TW
                sq_guards = []
                for gidx in range(NG):
                    buf = gidx % 2
                    gsnap = g_stage[buf].snap()
                    def a_load(sy, gidx=gidx, buf=buf, T0=T0, gsnap=gsnap, first=(ti == 0 and gidx == 0)):
                        W(sy, gsnap)
                        src = xT[:, T0:T0 + TW].rearrange("(a p) t -> p a t", p=128)[:, 2 * gidx:2 * gidx + 2, :]
                        sy.dma_start(A_xst[buf], src).then_inc(SEM["ax%d" % buf], 16)
                    em("sync", a_load)
                    inc("ax%d" % buf, 16)
                    g_ld = gd("ax%d" % buf)

                    gx1 = g_x1war.snap()
                    def a_copy(v, gidx=gidx, buf=buf, tt=tt, g_ld=g_ld, gx1=gx1):
                        W(v, g_ld)
                        W(v, gx1)
                        for k in range(2):
                            dc = 2 * gidx + k
                            v.tensor_copy(A_x1[:, dc, tt * TW:(tt + 1) * TW],
                                          A_xst[buf][:, k, :])
                        v.fin(SEM["vec_a"], 1)
                    em("vector", a_copy)
                    inc("vec_a", 1)
                    g_cp = gd("vec_a")

                    gsq = g_sq_cons[buf].snap()
                    def a_sqr(s, buf=buf, g_ld=g_ld, gsq=gsq):
                        W(s, g_ld)
                        W(s, gsq)
                        for k in range(2):
                            s.activation(A_sq[buf][:, k, :], A_xst[buf][:, k, :], AF.Square)
                        s.fin(SEM["sca_a"], 1)
                    em("scalar", a_sqr)
                    inc("sca_a", 1)
                    g_sqd = gd("sca_a")

                    gps = g_psAB.snap() if gidx == 0 else None
                    gini = G_INIT.snap() if (ti == 0 and gidx == 0) else None
                    def a_stat(t, gidx=gidx, buf=buf, tt=tt, g_cp=g_cp, g_sqd=g_sqd, gps=gps, gini=gini):
                        W(t, g_cp)
                        W(t, g_sqd)
                        if gini is not None:
                            W(t, gini)
                        if gps is not None:
                            W(t, gps)
                        for k in range(2):
                            dc = 2 * gidx + k
                            st = (gidx == 0 and k == 0)
                            sp = (gidx == NG - 1 and k == 1)
                            t.matmul(P0[:], ones_m[:], A_x1[:, dc, tt * TW:(tt + 1) * TW],
                                     start=st, stop=sp)
                            t.matmul(P1[:], ones_m[:], A_sq[buf][:, k, :],
                                     start=st, stop=sp)
                        t.fin(SEM["ten_a"], 1)
                    em("tensor", a_stat)
                    inc("ten_a", 1)
                    g_stage[buf] = Gd(SEM["ten_a"], val["ten_a"])
                    g_sq_cons[buf] = Gd(SEM["ten_a"], val["ten_a"])

                g_statall = gd("ten_a")
                def a_mid(v, g_statall=g_statall):
                    W(v, g_statall)
                    v.tensor_scalar(A_mu, P0[:], 1.0 / HID, None, ALU.mult)
                    v.tensor_scalar(A_ms, P1[:], 1.0 / HID, None, ALU.mult)
                    v.tensor_mul(A_rs, A_mu, A_mu)
                    v.tensor_sub(A_rs, A_ms, A_rs)
                    v.fin(SEM["vec_a"], 1)
                em("vector", a_mid)
                inc("vec_a", 1)
                g_psAB = gd("vec_a")
                g_mid = gd("vec_a")

                ginis = G_INIT.snap() if ti == 0 else None
                def a_rstd(s, g_mid=g_mid, ginis=ginis):
                    W(s, g_mid)
                    if ginis is not None:
                        W(s, ginis)
                    s.activation(A_rs, A_rs, AF.Ln, bias=eps_s[:, 0:1])
                    s.activation(A_rs, A_rs, AF.Exp, scale=-0.5)
                    s.fin(SEM["sca_a"], 1)
                em("scalar", a_rstd)
                inc("sca_a", 1)
                g_rstd = gd("sca_a")

                gm1 = g_mu1st.snap()
                def a_x1fin(v, tt=tt, g_rstd=g_rstd, gm1=gm1):
                    W(v, g_rstd)
                    W(v, gm1)
                    v.tensor_copy(t1_s[:], A_mu[0:1, :])
                    for dc in range(NDC):
                        xv = A_x1[:, dc, tt * TW:(tt + 1) * TW]
                        v.tensor_sub(A_f32s, xv, A_mu)
                        v.tensor_mul(xv, A_f32s, A_rs)
                    v.fin(SEM["vec_a"], 1)
                em("vector", a_x1fin)
                inc("vec_a", 1)
                g_x1tiles.append(gd("vec_a"))
                g_fin = gd("vec_a")

                def a_mu1st(sy, T0=T0, g_fin=g_fin):
                    W(sy, g_fin)
                    sy.dma_start(mu1sc[:, T0:T0 + TW], t1_s[:]).then_inc(SEM["am"], 16)
                em("sync", a_mu1st)
                inc("am", 16)
                g_mu1st = gd("am")

            for cc in range(NCC if RUN_A2 else 0):
                ci = sb * NCC + cc
                wbuf = ci % 2
                gw = g_wpan[wbuf].snap()
                def a_wld(g, cc=cc, wbuf=wbuf, gw=gw):
                    W(g, gw)
                    g.dma_start(A_wp[wbuf],
                                wQKV[cc].rearrange("(a p) c -> p a c", p=128)).then_inc(SEM["aw%d" % wbuf], 16)
                em("gpsimd", a_wld)
                inc("aw%d" % wbuf, 16)
                g_w = [gd("aw%d" % wbuf), gd("iw")]

                banks = [PS[2], PS[3], PS[4], PS[5]] if cc % 2 == 0 else [PS[0], PS[1], PS[6], PS[7]]
                gx1all = g_x1tiles[sb * 4 + 3].snap()
                gbk = g_qkv_banks[cc % 2].snap()
                def a_qkvmm(t, wbuf=wbuf, banks=banks, g_w=g_w, gx1all=gx1all, gbk=gbk):
                    W(t, g_w)
                    W(t, gx1all)
                    W(t, gbk)
                    for dc in range(NDC):
                        for j in range(4):
                            t.matmul(banks[j][:], A_wp[wbuf][:, dc, :],
                                     A_x1[:, dc, j * TW:(j + 1) * TW],
                                     start=(dc == 0), stop=(dc == NDC - 1))
                    t.fin(SEM["ten_a"], 1)
                em("tensor", a_qkvmm)
                inc("ten_a", 1)
                g_wpan[wbuf] = gd("ten_a")
                g_mmd = gd("ten_a")

                gst = g_qst[ci % 2].snap()
                def a_qkvev(v, cc=cc, ci=ci, banks=banks, g_mmd=g_mmd, gst=gst):
                    W(v, g_mmd)
                    W(v, gst)
                    isq = cc < 16
                    for j in range(4):
                        if isq:
                            v.tensor_scalar(A_qst[ci % 2][:, j, :], banks[j][:],
                                            bqkv_s[:, cc:cc + 1], 1.0 / 128.0, ALU.add, ALU.mult)
                        else:
                            v.tensor_scalar(A_qst[ci % 2][:, j, :], banks[j][:],
                                            bqkv_s[:, cc:cc + 1], None, ALU.add)
                    v.fin(SEM["vec_a"], 1)
                em("vector", a_qkvev)
                inc("vec_a", 1)
                g_qkv_banks[cc % 2] = gd("vec_a")
                g_ev = gd("vec_a")

                def a_qkvst(sy, cc=cc, ci=ci, sb=sb, g_ev=g_ev):
                    W(sy, g_ev)
                    sy.dma_start(qkvT[cc * 128:(cc + 1) * 128, sb * 2048:(sb + 1) * 2048]
                                 .rearrange("p (j t) -> p j t", j=4),
                                 A_qst[ci % 2]).then_inc(SEM["as%d" % (ci % 2)], 16)
                em("sync", a_qkvst)
                inc("as%d" % (ci % 2), 16)
                g_qst[ci % 2] = gd("as%d" % (ci % 2))

            g_x1war = gd("ten_a")
            g_psAB = gd("vec_a")

        G_AQKV = [gd("as0"), gd("as1"), gd("am")]

        def b_t3zero(g, gaq=[g.snap() for g in G_AQKV]):
            W(g, gaq)
            g.memset(B_t3z[:, :], 0.0)
            g.fin(SEM["btz"], 1)
        em("gpsimd", b_t3zero)
        inc("btz", 1)
        G_BTZ = gd("btz")

        def wconv(g):
            for i in range(8):
                g.dma_start(wPbf[i * 512:(i + 1) * 512, :],
                            wP[i * 512:(i + 1) * 512, :]).then_inc(SEM["wcv"], 16)
            for i in range(8):
                g.dma_start(wFbf.rearrange("c h w -> (c h) w")[i * 16384 // 128 * 32:(i + 1) * 4096, :] if False else
                            wFbf[i * 4:(i + 1) * 4],
                            wFT[i * 4:(i + 1) * 4]).then_inc(SEM["wcv"], 16)
        em("gpsimd", wconv)
        inc("wcv", 16 * 16)
        G_WCV = gd("wcv")

        RUN_B = "B" in phases
        RUN_C = "C" in phases
        RUN_D = "D" in phases
        g_kbuf = Gd()
        g_qbuf = [Gd(), Gd()]
        g_sxbuf = [Gd() for _ in range(8)]
        g_ptbuf = [Gd() for _ in range(8)]
        g_scoreps = [Gd(), Gd()]
        g_p2 = Gd()
        g_p3 = Gd()
        g_p4 = Gd()
        g_p5 = Gd()
        g_oebuf = [Gd(), Gd()]
        g_kn = Gd()

        SKW = 3
        g_po = [Gd(), Gd()]
        g_pd = [Gd(), Gd()]
        PO = [P2, P6]
        PD = [P3, P7]
        pending_epi = [None]
        pend_pv = []
        cglob = [0]
        g_brc = Gd()
        g_bqs = Gd()
        g_p45 = [Gd(), Gd()]
        PP45 = [P4, P5]
        g_mx = [Gd(), Gd()]
        g_ballg = [Gd(), Gd()]
        g_t3 = [Gd(), Gd()]
        g_ncg = [[Gd() for _ in range(8)], [Gd() for _ in range(8)]]
        headinfo = {}

        def emit_kvload(h):
            par = h % 2
            gkv = [gd("ten_b"), gd("sca_b"), gd("vec_b")]
            gaq = [g.snap() for g in G_AQKV] if h == 0 else None
            def b_kld(sy, h=h, par=par, gkv=gkv, gaq=gaq):
                if gaq is not None:
                    W(sy, gaq)
                W(sy, gkv)
                sy.dma_start(B_kT[par], qkvT[2048 + h * 128: 2048 + (h + 1) * 128, :]).then_inc(SEM["bk"], 16)
                sy.dma_start(B_qh[par], qkvT[h * 128:(h + 1) * 128, :]).then_inc(SEM["bq%d" % par], 16)
            em("sync", b_kld)
            inc("bk", 16)
            inc("bq%d" % par, 16)
            gk = gd("bk")
            gq = gd("bq%d" % par)
            def b_vld(sy, h=h, par=par, gkv=gkv, gaq=gaq):
                if gaq is not None:
                    W(sy, gaq)
                W(sy, gkv)
                sy.dma_start(B_v[par],
                             qkvT[4096 + h * 128: 4096 + (h + 1) * 128, :]
                             .rearrange("p (c q) -> p c q", c=NDC),
                             transpose=True).then_inc(SEM["bv"], 16)
            em("sync", b_vld)
            inc("bv", 16)
            return gk, gq, gd("bv")

        def emit_headprep(h):
            nonlocal g_bqs
            par = h % 2
            g_kd, g_qld, g_vdx = headinfo[h]["loads"]

            def b_kmx(v, par=par, g_kd=g_kd):
                W(v, g_kd)
                v.reduce_max(kmx_b[:], B_kT[par], AXX, apply_absolute_value=True)
                v.bar()
                v.tensor_mul(kmx_b[:], kmx_b[:], kmx_b[:])
                v.fin(SEM["vec_b"], 1)
            em("vector", b_kmx)
            inc("vec_b", 1)
            g_kmx = gd("vec_b")

            gp = g_p45[0].snap()
            def b_kn2(t, g_kmx=g_kmx, gp=gp):
                W(t, g_kmx)
                W(t, gp)
                t.matmul(P4[0:1, 0:1], ones_c[:], kmx_b[:], start=True, stop=True).then_inc(SEM["ten_b"], 1)
            em("tensor", b_kn2)
            inc("ten_b", 1)
            g_kn2 = gd("ten_b")

            gmx = g_mx[par].snap()
            def b_mx(v, par=par, g_kn2=g_kn2, gmx=gmx):
                W(v, g_kn2)
                W(v, gmx)
                v.tensor_scalar(mx_s[0:1, par:par + 1], P4[0:1, 0:1], c256_s[0:1, 0:1], 1.0,
                                ALU.mult, ALU.add)
                v.fin(SEM["vec_b"], 1)
            em("vector", b_mx)
            inc("vec_b", 1)
            g_mxs = gd("vec_b")
            g_p45[0] = gd("vec_b")

            gbl = g_ballg[par].snap()
            def b_ball(s, h=h, par=par, g_kd=g_kd, gbl=gbl):
                W(s, g_kd)
                W(s, gbl)
                s.mul(ballg[:, par * NDC:(par + 1) * NDC], kvec_s[:], slp_s[:, h:h + 1])
                s.fin(SEM["sca_b"], 1)
            em("scalar", b_ball)
            inc("sca_b", 1)
            g_balld = gd("sca_b")

            gbq = g_bqs.snap()
            def b_qsq(s, par=par, g_qld=g_qld, gbq=gbq):
                W(s, g_qld)
                W(s, gbq)
                s.activation(B_qs, B_qh[par], AF.Square)
                s.fin(SEM["sca_b"], 1)
            em("scalar", b_qsq)
            inc("sca_b", 1)
            g_qsqd = gd("sca_b")

            for i in range(8):
                gp = g_p45[i % 2].snap()
                def b_qn2(t, i=i, par=par, g_qsqd=g_qsqd, gp=gp):
                    if i == 0:
                        W(t, g_qsqd)
                    W(t, gp)
                    t.matmul(PP45[i % 2][0:1, :], ones_c[:], B_qs[:, i * 512:(i + 1) * 512],
                             start=True, stop=True).then_inc(SEM["ten_b"], 1)
                em("tensor", b_qn2)
                inc("ten_b", 1)
                g_q2 = gd("ten_b")

                def b_qnc(v, i=i, par=par, g_q2=g_q2):
                    W(v, g_q2)
                    v.tensor_copy(B_qn[0:1, i * 512:(i + 1) * 512], PP45[i % 2][0:1, :])
                    v.fin(SEM["vec_b"], 1)
                em("vector", b_qnc)
                inc("vec_b", 1)
                g_p45[i % 2] = gd("vec_b")
            g_bqs = gd("ten_b")
            g_qnd = gd("vec_b")

            gt3 = g_t3[par].snap()
            def b_crow(v, h=h, par=par, g_qnd=g_qnd, g_mxs=g_mxs, gt3=gt3):
                W(v, g_mxs)
                W(v, gt3)
                v.tensor_scalar(B_qn[0:1, :], B_qn[0:1, :], c64_s[0:1, 0:1], mx_s[0:1, par:par + 1],
                                ALU.mult, ALU.add)
                v.bar()
                v.scalar_tensor_tensor(B_t3[par][0:1, :], B_qvf[0:1, :], nslp_s[0:1, h:h + 1], B_qn[0:1, :],
                                       ALU.mult, ALU.subtract)
                v.fin(SEM["vec_b"], 1)
            em("vector", b_crow)
            inc("vec_b", 1)
            headinfo[h]["prep"] = gd("vec_b")
            headinfo[h]["ball"] = g_balld

        def emit_epilogue_mk(hl, qt, qp, qb, g_pvall):
            def doit():
                gbrc = g_brc.snap()
                def b_recip(s, qp=qp, g_pvall=g_pvall, gbrc=gbrc):
                    W(s, g_pvall)
                    W(s, gbrc)
                    s.activation(B_rc, PD[qp][:], AF.Ln)
                    s.activation(B_rc, B_rc, AF.Exp, scale=-1.0)
                    s.fin(SEM["sca_b"], 1)
                em("scalar", b_recip)
                inc("sca_b", 1)
                g_rec = gd("sca_b")
                g_pd[qp].sem, g_pd[qp].val = SEM["sca_b"], val["sca_b"]

                goe = g_oebuf[qb].snap()
                def b_oev(v, qp=qp, qb=qb, g_rec=g_rec, goe=goe):
                    W(v, g_rec)
                    W(v, goe)
                    v.tensor_mul(B_oe[qb], PO[qp][:], B_rc)
                    v.fin(SEM["vec_b"], 1)
                em("vector", b_oev)
                inc("vec_b", 1)
                g_brc.sem, g_brc.val = SEM["vec_b"], val["vec_b"]
                g_po[qp].sem, g_po[qp].val = SEM["vec_b"], val["vec_b"]
                g_oed = gd("vec_b")

                def b_ost(sy, hl=hl, qt=qt, qb=qb, g_oed=g_oed):
                    W(sy, g_oed)
                    sy.dma_start(ag_in.ap()[hl * 128:(hl + 1) * 128, qt * TW:(qt + 1) * TW],
                                 B_oe[qb]).then_inc(SEM["bo%d" % qb], 16)
                em("sync", b_ost)
                inc("bo%d" % qb, 16)
                g_oebuf[qb].sem, g_oebuf[qb].val = SEM["bo%d" % qb], val["bo%d" % qb]
            return doit

        def flush_pv(n):
            while len(pend_pv) > n:
                pend_pv.pop(0)()

        if RUN_B:
            def b_qvf(sy, gaq=[g.snap() for g in G_AQKV], gz=G_BTZ.snap()):
                W(sy, gaq)
                W(sy, gz)
                sy.dma_start(B_qvf[0:1, :], qveci).then_inc(SEM["bqv"], 16)
            em("sync", b_qvf)
            inc("bqv", 16)
            g_qvf = gd("bqv")
            g_ncg = [[Gd(sem=None) for _ in range(8)] for _ in range(2)]
            g_ncg = [[Gd() for _ in range(8)], [Gd() for _ in range(8)]]
            headinfo[0] = {"loads": emit_kvload(0)}
            g_t3[0] = g_qvf
            g_t3[1] = g_qvf
            emit_headprep(0)

        for hl in range(NHC if RUN_B else 0):
            par = hl % 2
            g_kd, g_qld, g_vd = headinfo[hl]["loads"]
            g_prep = headinfo[hl]["prep"]
            g_balld = headinfo[hl]["ball"]

            for qt in range(8):
                nch = 4 * (qt + 1)
                qb = qt % 2
                qp = qt % 2
                exp_g = {}
                done_next = [False]

                for k in range(nch):
                    cg = cglob[0] % 8
                    cglob[0] += 1
                    cb2 = k % 2
                    dd = k - (nch - 4)
                    masked = dd >= 0
                    sxi = k % 4
                    gsc = g_scoreps[cb2].snap()
                    def b_mms(t, hl=hl, par=par, qt=qt, k=k, cb2=cb2, gsc=gsc,
                              gq=(g_qld.snap() if (qt == 0 and k == 0) else None),
                              gpp=(g_prep.snap() if (qt == 0 and k == 0) else None)):
                        W(t, gsc)
                        if gq is not None:
                            W(t, gq)
                        if gpp is not None:
                            W(t, gpp)
                        t.matmul(PS[cb2][:], B_kT[par][:, k * 128:(k + 1) * 128],
                                 B_qh[par][:, qt * TW:(qt + 1) * TW],
                                 start=True, stop=False)
                        t.matmul(PS[cb2][:], ones_m[:], B_t3[par][:, qt * TW:(qt + 1) * TW],
                                 start=False, stop=True).then_inc(SEM["ten_b"], 1)
                    em("tensor", b_mms)
                    inc("ten_b", 1)
                    g_sd = gd("ten_b")

                    col = par * NDC + k
                    if masked:
                        gsx = g_sxbuf[sxi].snap()
                        def b_msk(v, cb2=cb2, sxi=sxi, dd=dd, g_sd=g_sd, gsx=gsx):
                            W(v, g_sd)
                            W(v, gsx)
                            v.tensor_add(B_sx[sxi], PS[cb2][:], md_s[:, dd, :])
                            v.fin(SEM["vec_b"], 1)
                        em("vector", b_msk)
                        inc("vec_b", 1)
                        g_mskd = gd("vec_b")
                        g_scoreps[cb2] = g_mskd

                    gpt = g_ptbuf[cg].snap()
                    if masked:
                        def b_exp(s, cg=cg, sxi=sxi, col=col, g_mskd=g_mskd, gpt=gpt):
                            W(s, g_mskd)
                            W(s, gpt)
                            s.activation(B_pt[cg], B_sx[sxi], AF.Exp, bias=ballg[:, col:col + 1])
                            s.fin(SEM["sca_b"], 1)
                    else:
                        def b_exp(s, cg=cg, cb2=cb2, col=col, g_sd=g_sd, gpt=gpt):
                            W(s, g_sd)
                            W(s, gpt)
                            s.activation(B_pt[cg], PS[cb2][:], AF.Exp, bias=ballg[:, col:col + 1])
                            s.fin(SEM["sca_b"], 1)
                    em("scalar", b_exp)
                    inc("sca_b", 1)
                    if masked:
                        g_sxbuf[sxi] = gd("sca_b")
                    else:
                        g_scoreps[cb2] = gd("sca_b")
                    exp_g[k] = gd("sca_b")

                    def mk_pv(par=par, qp=qp, j=k, cg=cg, g_expd=exp_g[k],
                              g_vdx=(g_vd.snap() if (qt == 0 and k == 0) else None),
                              last=(k == nch - 1), hl=hl, qt=qt, qb=qb):
                        def do_pv():
                            if j == 0 and pending_epi[0] is not None:
                                pending_epi[0]()
                                pending_epi[0] = None
                            gpox = g_po[qp].snap() if j == 0 else None
                            gpdx = g_pd[qp].snap() if j == 0 else None
                            def b_pv(t, par=par, qp=qp, j=j, cg=cg, g_expd=g_expd, g_vdx=g_vdx,
                                     gpox=gpox, gpdx=gpdx, last=last):
                                W(t, g_expd)
                                if g_vdx is not None:
                                    W(t, g_vdx)
                                if gpox is not None:
                                    W(t, gpox)
                                    W(t, gpdx)
                                t.matmul(PO[qp][:], B_v[par][:, j, :], B_pt[cg],
                                         start=(j == 0), stop=last)
                                t.matmul(PD[qp][:], ones_m[:], B_pt[cg],
                                         start=(j == 0), stop=last).then_inc(SEM["ten_b"], 1)
                            em("tensor", b_pv)
                            inc("ten_b", 1)
                            g_ptbuf[cg].sem, g_ptbuf[cg].val = SEM["ten_b"], val["ten_b"]
                            if last:
                                assert pending_epi[0] is None
                                pending_epi[0] = emit_epilogue_mk(hl, qt, qp, qb, gd("ten_b"))
                        return do_pv
                    pend_pv.append(mk_pv())
                    flush_pv(SKW)

                    if qt == 2 and k == 0 and hl + 1 < NHC and (hl + 1) not in headinfo:
                        headinfo[hl + 1] = {"loads": emit_kvload(hl + 1)}
                    if qt == 4 and k == 0 and hl + 1 < NHC and "prep" not in headinfo.get(hl + 1, {}):
                        emit_headprep(hl + 1)
            g_t3[par] = gd("ten_b")

        if RUN_B:
            flush_pv(0)
            if pending_epi[0] is not None:
                pending_epi[0]()
                pending_epi[0] = None

        G_BOST = [gd("bo0"), gd("bo1")]

        if not RUN_B:
            RUN_C = False
        def b_ag(g, G_BOST=[g.snap() for g in G_BOST]):
            W(g, G_BOST)
            g.collective_compute("AllGather", ALU.bypass,
                                 replica_groups=[[0, 1], [2, 3], [4, 5], [6, 7]],
                                 ins=[ag_in.ap().opt()], outs=[ag_out.ap().opt()]).then_inc(SEM["coll"], 1)
        if RUN_C:
            em("gpsimd", b_ag)
            inc("coll", 1)
        G_AG = gd("coll")

        g_oThalf = Gd()
        g_wpanC = [Gd(), Gd()]
        g_projps = [Gd(), Gd()]
        g_p45C = Gd()
        g_p6C = Gd()
        g_xrbuf = [Gd(), Gd()]
        g_sqsbuf = [Gd(), Gd()]
        g_x2tile = Gd()
        g_at_s = Gd()

        for hf in range(2 if RUN_C else 0):
            goh = [g.snap() for g in g_oThalf] if isinstance(g_oThalf, list) else g_oThalf.snap()
            gag = G_AG.snap()
            def c_old(sy, hf=hf, goh=goh, gag=gag):
                W(sy, gag)
                W(sy, goh)
                for arm in sy.switch_core_id(n=128):
                    g_id = int(arm.logical) % 2
                    base = g_id * 2048 + hf * 1024
                    sy.dma_start(C_oT, ag_out.ap()[:, base:base + 1024]
                                 .rearrange("(c p) t -> p c t", p=128)).then_inc(SEM["co"], 16)
            em("sync", c_old)
            inc("co", 16)
            g_ohd = gd("co")

            for oc in range(NDC):
                oi = hf * NDC + oc
                gwp = g_wpanC[oc % 2].snap()
                def c_wld(g, oc=oc, gwp=gwp, gwc=(G_WCV.snap() if oi <= 1 else None)):
                    W(g, gwp)
                    if gwc is not None:
                        W(g, gwc)
                    g.dma_start(C_wp[oc % 2], wPbf[:, oc * 128:(oc + 1) * 128]
                                .rearrange("(a p) c -> p a c", p=128)).then_inc(SEM["cw%d" % (oc % 2)], 16)
                em("gpsimd", c_wld)
                inc("cw%d" % (oc % 2), 16)
                g_wd = [gd("cw%d" % (oc % 2)), gd("iw")]

                banks = [PS[0], PS[1]] if oc % 2 == 0 else [PS[2], PS[3]]
                gpp = g_projps[oc % 2].snap()
                def c_mm(t, oc=oc, banks=banks, g_wd=g_wd, g_ohd=g_ohd, gpp=gpp):
                    W(t, g_wd)
                    W(t, g_ohd)
                    W(t, gpp)
                    for dc in range(NDC):
                        for j in range(2):
                            t.matmul(banks[j][:], C_wp[oc % 2][:, dc, :],
                                     C_oT[:, dc, j * TW:(j + 1) * TW],
                                     start=(dc == 0), stop=(dc == NDC - 1))
                    t.fin(SEM["ten_c"], 1)
                em("tensor", c_mm)
                inc("ten_c", 1)
                g_wpanC[oc % 2] = gd("ten_c")
                g_mmd = gd("ten_c")

                gat = g_at_s.snap() if (oc == 0) else None
                def c_ev(v, oc=oc, banks=banks, g_mmd=g_mmd, gat=gat):
                    W(v, g_mmd)
                    if gat is not None:
                        W(v, gat)
                    for j in range(2):
                        v.tensor_scalar(C_at[:, oc, j * TW:(j + 1) * TW], banks[j][:],
                                        bp_s[:, oc:oc + 1], None, ALU.add)
                    v.fin(SEM["vec_c"], 1)
                em("vector", c_ev)
                inc("vec_c", 1)
                g_projps[oc % 2] = gd("vec_c")

            g_evall = gd("vec_c")

            for lt in range(2):
                tglob = hf * 2 + lt
                gp45 = g_p45C.snap()
                sq_guard_list = []
                for dc in range(NDC):
                    gsq = g_sqsbuf[dc % 2].snap()
                    def c_sq(s, dc=dc, lt=lt, g_evall=g_evall, gsq=gsq):
                        if dc == 0:
                            W(s, g_evall)
                        W(s, gsq)
                        s.activation(C_sqs[dc % 2], C_at[:, dc, lt * TW:(lt + 1) * TW],
                                     AF.Square)
                        s.fin(SEM["sca_c"], 1)
                    em("scalar", c_sq)
                    inc("sca_c", 1)
                    g_sqd = gd("sca_c")

                    def c_st(t, dc=dc, lt=lt, g_sqd=g_sqd, gp45=(gp45 if dc == 0 else None)):
                        W(t, g_sqd)
                        if gp45 is not None:
                            W(t, gp45)
                        t.matmul(P4[:], ones_m[:], C_at[:, dc, lt * TW:(lt + 1) * TW],
                                 start=(dc == 0), stop=(dc == NDC - 1))
                        t.matmul(P5[:], ones_m[:], C_sqs[dc % 2],
                                 start=(dc == 0), stop=(dc == NDC - 1)).then_inc(SEM["ten_c"], 1)
                    em("tensor", c_st)
                    inc("ten_c", 1)
                    g_sqsbuf[dc % 2] = gd("ten_c")

                g_st2 = gd("ten_c")
                def c_mid(v, g_st2=g_st2):
                    W(v, g_st2)
                    v.tensor_scalar(C_mu, P4[:], 1.0 / HID, None, ALU.mult)
                    v.tensor_scalar(C_ms, P5[:], 1.0 / HID, None, ALU.mult)
                    v.tensor_mul(C_rs, C_mu, C_mu)
                    v.tensor_sub(C_rs, C_ms, C_rs)
                    v.fin(SEM["vec_c"], 1)
                em("vector", c_mid)
                inc("vec_c", 1)
                g_p45C = gd("vec_c")
                g_mid2 = gd("vec_c")

                def c_rstd(s, g_mid2=g_mid2):
                    W(s, g_mid2)
                    s.activation(C_rs, C_rs, AF.Ln, bias=eps_s[:, 0:1])
                    s.activation(C_rs, C_rs, AF.Exp, scale=-0.5)
                    s.fin(SEM["sca_c"], 1)
                em("scalar", c_rstd)
                inc("sca_c", 1)
                g_rstd2 = gd("sca_c")

                def c_m1ld(sy, tglob=tglob, g_rstd2=g_rstd2):
                    W(sy, g_rstd2)
                    for arm in sy.switch_core_id(n=128):
                        g_id = int(arm.logical) % 2
                        gbase = g_id * 2048 + tglob * TW
                        sy.dma_start(mu1_s[:], mu1sc[:, gbase:gbase + TW]).then_inc(SEM["cm%d" % (tglob % 2)], 16)
                    sy.dma_start(rmk_s[:], rmaski[:, tglob * TW:(tglob + 1) * TW]).then_inc(SEM["cm%d" % (tglob % 2)], 16)
                em("sync", c_m1ld)
                inc("cm%d" % (tglob % 2), 32)
                g_m1d = gd("cm%d" % (tglob % 2))

                gp6 = g_p6C.snap()
                def c_m1b(t, g_m1d=g_m1d, gp6=gp6):
                    W(t, g_m1d)
                    W(t, gp6)
                    t.matmul(P6[:], ones_rf[:], mu1_s[:], start=True, stop=True).then_inc(SEM["ten_c"], 1)
                em("tensor", c_m1b)
                inc("ten_c", 1)
                g_m1b = gd("ten_c")

                def c_m1c(v, g_m1b=g_m1b):
                    W(v, g_m1b)
                    v.tensor_copy(C_ms, P6[:])
                    v.fin(SEM["vec_c"], 1)
                em("vector", c_m1c)
                inc("vec_c", 1)
                g_m1c = gd("vec_c")

                def c_rmb(t, g_m1c=g_m1c):
                    W(t, g_m1c)
                    t.matmul(P6[:], ones_rf[:], rmk_s[:], start=True, stop=True).then_inc(SEM["ten_c"], 1)
                em("tensor", c_rmb)
                inc("ten_c", 1)
                g_rmbd = gd("ten_c")

                def c_rmc(v, g_rmbd=g_rmbd):
                    W(v, g_rmbd)
                    v.tensor_copy(C_rmb, P6[:])
                    v.fin(SEM["vec_c"], 1)
                em("vector", c_rmc)
                inc("vec_c", 1)
                g_p6C = gd("vec_c")

                gx2t = g_x2tile.snap()
                for dc in range(NDC):
                    xb = dc % 2
                    gxr = g_xrbuf[xb].snap()
                    def c_xld(sy, dc=dc, xb=xb, tglob=tglob, gxr=gxr):
                        W(sy, gxr)
                        sy.dma_start(C_xr[xb], xTown[dc * 128:(dc + 1) * 128,
                                                     tglob * TW:(tglob + 1) * TW]).then_inc(SEM["cx%d" % xb], 16)
                    em("sync", c_xld)
                    inc("cx%d" % xb, 16)
                    g_xd = gd("cx%d" % xb)

                    def c_x2(v, dc=dc, xb=xb, lt=lt, g_xd=g_xd, gx2t=(gx2t if dc == 0 else None)):
                        W(v, g_xd)
                        if gx2t is not None:
                            W(v, gx2t)
                        v.tensor_sub(C_f1, C_at[:, dc, lt * TW:(lt + 1) * TW], C_mu)
                        v.tensor_mul(C_f1, C_f1, C_rs)
                        v.tensor_add(C_x2[:, dc, :], C_f1, C_xr[xb])
                        v.fin(SEM["vec_c"], 1)
                    em("vector", c_x2)
                    inc("vec_c", 1)
                    g_xrbuf[xb] = gd("vec_c")
                    g_x2d = gd("vec_c")

                    def c_x2sp(sy, dc=dc, tglob=tglob, g_x2d=g_x2d):
                        W(sy, g_x2d)
                        sy.dma_start(x2sc[dc * 128:(dc + 1) * 128, tglob * TW:(tglob + 1) * TW],
                                     C_x2[:, dc, :]).then_inc(SEM["cxs"], 16)
                    em("sync", c_x2sp)
                    inc("cxs", 16)
                g_x2tile = gd("cxs")

                gp45b = g_p45C.snap()
                for dc in range(NDC):
                    gsq2 = g_sqsbuf[0].snap()
                    def c_x2b(v, dc=dc, gsq2=gsq2):
                        W(v, gsq2)
                        v.tensor_copy(C_x2b, C_x2[:, dc, :])
                        v.fin(SEM["vec_c"], 1)
                    em("vector", c_x2b)
                    inc("vec_c", 1)
                    g_cb = gd("vec_c")

                    def c_sq3(s, g_cb=g_cb):
                        W(s, g_cb)
                        s.activation(C_sqs[0], C_x2b, AF.Square)
                        s.fin(SEM["sca_c"], 1)
                    em("scalar", c_sq3)
                    inc("sca_c", 1)
                    g_sq3 = gd("sca_c")

                    def c_st3(t, dc=dc, g_sq3=g_sq3, gp45b=(gp45b if dc == 0 else None)):
                        W(t, g_sq3)
                        if gp45b is not None:
                            W(t, gp45b)
                        t.matmul(P5[:], ones_m[:], C_sqs[0],
                                 start=(dc == 0), stop=(dc == NDC - 1)).then_inc(SEM["ten_c"], 1)
                    em("tensor", c_st3)
                    inc("ten_c", 1)
                    g_sqsbuf[0] = gd("ten_c")

                g_st3 = gd("ten_c")
                def c_mid3(v, g_st3=g_st3):
                    W(v, g_st3)
                    v.tensor_scalar(C_f1, P5[:], 1.0 / HID, None, ALU.mult)
                    v.tensor_mul(C_rs, C_ms, C_ms)
                    v.tensor_sub(C_rs, C_f1, C_rs)
                    v.fin(SEM["vec_c"], 1)
                em("vector", c_mid3)
                inc("vec_c", 1)
                g_p45C = gd("vec_c")
                g_mid3 = gd("vec_c")

                def c_rstd3(s, g_mid3=g_mid3):
                    W(s, g_mid3)
                    s.activation(C_rs, C_rs, AF.Ln, bias=eps_s[:, 0:1])
                    s.activation(C_rs, C_rs, AF.Exp, scale=-0.5)
                    s.fin(SEM["sca_c"], 1)
                em("scalar", c_rstd3)
                inc("sca_c", 1)
                g_rstd3 = gd("sca_c")

                for dc in range(NDC):
                    gsq3 = g_sqsbuf[dc % 2].snap()
                    def c_x3(v, dc=dc, tglob=tglob, g_rstd3=g_rstd3, gsq3=gsq3,
                             GI=(G_INIT.snap() if tglob == 0 and dc == 0 else None)):
                        if GI is not None:
                            W(v, GI)
                        if dc == 0:
                            W(v, g_rstd3)
                        W(v, gsq3)
                        v.tensor_sub(C_f1, C_x2[:, dc, :], C_ms)
                        v.tensor_mul(C_sqs[dc % 2], C_f1, C_rs)
                        v.tensor_mul(C_f1, C_sqs[dc % 2], C_rmb)
                        v.reduce_sum(rqp_s[:, dc, tglob:tglob + 1], C_f1, AXX)
                        v.fin(SEM["vec_c"], 1)
                    em("vector", c_x3)
                    inc("vec_c", 1)
                    g_x3d = gd("vec_c")

                    def c_x3sp(sy, dc=dc, tglob=tglob, g_x3d=g_x3d):
                        W(sy, g_x3d)
                        sy.dma_start(x3sc[dc * 128:(dc + 1) * 128, tglob * TW:(tglob + 1) * TW],
                                     C_sqs[dc % 2]).then_inc(SEM["c3%d" % (dc % 2)], 16)
                    em("sync", c_x3sp)
                    inc("c3%d" % (dc % 2), 16)
                    g_sqsbuf[dc % 2] = gd("c3%d" % (dc % 2))

            g_oThalf = [gd("c30"), gd("c31")]
            g_at_s = gd("sca_c")
        G_CX3 = [gd("c30"), gd("c31"), gd("cxs")]

        if not RUN_C:
            RUN_D = False
        RTR = RUN_C
        g_vlast = gd("vec_c")
        def r_sum(v, g_vlast=g_vlast, GI=G_INIT.snap()):
            W(v, GI)
            v.tensor_add(rqs_s[:], rqp_s[:, :, 0], rqp_s[:, :, 1])
            v.tensor_add(rqs_s[:], rqs_s[:], rqp_s[:, :, 2])
            v.tensor_add(rqs_s[:], rqs_s[:], rqp_s[:, :, 3])
            v.fin(SEM["vec_c"], 1)
        if RTR:
            em("vector", r_sum)
            inc("vec_c", 1)
        g_rsum = gd("vec_c")

        def r_spill(sy, g_rsum=g_rsum):
            W(sy, g_rsum)
            sy.dma_start(rq_in.ap(), rqs_s[:]).then_inc(SEM["rsp"], 16)
        if RTR:
            em("sync", r_spill)
            inc("rsp", 16)
        g_rsp = gd("rsp")

        def r_ar(g, g_rsp=g_rsp):
            W(g, g_rsp)
            g.collective_compute("AllReduce", ALU.add,
                                 replica_groups=[[0, 1], [2, 3], [4, 5], [6, 7]],
                                 ins=[rq_in.ap().opt()], outs=[rq_out.ap().opt()]).then_inc(SEM["coll"], 1)
        if RTR:
            em("gpsimd", r_ar)
            inc("coll", 1)
        g_rar = gd("coll")

        def r_load(sy, g_rar=g_rar):
            W(sy, g_rar)
            sy.dma_start(rqs_s[:], rq_out.ap()).then_inc(SEM["rld"], 16)
        if RTR:
            em("sync", r_load)
            inc("rld", 16)
        g_rld = gd("rld")

        def r_b16(v, g_rld=g_rld):
            W(v, g_rld)
            v.tensor_scalar(rqb_s[:], rqs_s[:], 1.0 / 3484.0, None, ALU.mult)
            v.fin(SEM["vec_c"], 1)
        if RTR:
            em("vector", r_b16)
            inc("vec_c", 1)
        g_rb16 = gd("vec_c")

        gp6r = g_p6C.snap()
        def r_mm(t, g_rb16=g_rb16, gp6r=gp6r):
            W(t, g_rb16)
            W(t, gp6r)
            for dc in range(NDC):
                t.matmul(P6[0:1, 0:8], rqb_s[:, dc:dc + 1], wrt_s[:, dc, :],
                         start=(dc == 0), stop=(dc == NDC - 1))
            t.fin(SEM["ten_c"], 1)
        if RTR:
            em("tensor", r_mm)
            inc("ten_c", 1)
        g_rmm = gd("ten_c")

        def r_soft(v, g_rmm=g_rmm):
            W(v, g_rmm)
            v.tensor_add(t8_s[:], P6[0:1, 0:8], br_s[:])
            v.bar()
            v.reduce_max(t8b_s[:], t8_s[:], AXX)
            v.bar()
            v.tensor_scalar(t8_s[:], t8_s[:], t8b_s[0:1, 0:1], None, ALU.subtract)
            v.fin(SEM["vec_c"], 1)
        if RTR:
            em("vector", r_soft)
            inc("vec_c", 1)
        g_rsf = gd("vec_c")

        def r_exp(s, g_rsf=g_rsf):
            W(s, g_rsf)
            s.activation(t8_s[:], t8_s[:], AF.Exp)
            s.fin(SEM["sca_c"], 1)
        if RTR:
            em("scalar", r_exp)
            inc("sca_c", 1)
        g_rexp = gd("sca_c")

        def r_norm(v, g_rexp=g_rexp):
            W(v, g_rexp)
            v.reduce_sum(t8b_s[:], t8_s[:], AXX)
            v.bar()
            v.reciprocal(t8b_s[:], t8b_s[:])
            v.bar()
            v.tensor_scalar(t8_s[:], t8_s[:], t8b_s[0:1, 0:1], 2.0, ALU.mult, ALU.mult)
            v.bar()
            v.tensor_copy(t8c_s[:], t8_s[:])
            v.fin(SEM["vec_c"], 1)
        if RTR:
            em("vector", r_norm)
            inc("vec_c", 1)
        g_rnm = gd("vec_c")

        _rb_target = val["rsp"] + 16
        def r_bounce(sy, g_rnm=g_rnm, _rb_target=_rb_target):
            W(sy, g_rnm)
            sy.dma_start(r8dram, t8c_s[:]).then_inc(SEM["rsp"], 16)
            sy.wait_ge(SEM["rsp"], _rb_target)
            sy.dma_start(r8c_s[:], r8dram.rearrange("one e -> e one")).then_inc(SEM["rld"], 16)
        if RTR:
            em("sync", r_bounce)
            inc("rsp", 16)
            inc("rld", 16)
        g_rbn = gd("rld")

        def r_ermm(t, g_rbn=g_rbn):
            W(t, g_rbn)
            t.matmul(P6[:, 0:1], er8_s[:], r8c_s[:], start=True, stop=True).then_inc(SEM["ten_c"], 1)
        if RTR:
            em("tensor", r_ermm)
            inc("ten_c", 1)
        g_rem = gd("ten_c")

        def r_rsc(v, g_rem=g_rem):
            W(v, g_rem)
            v.tensor_copy(rsc_s[:], P6[:, 0:1])
            v.fin(SEM["vec_c"], 1)
        if RTR:
            em("vector", r_rsc)
            inc("vec_c", 1)
        G_RSC = gd("vec_c")
        g_p6C = gd("vec_c")

        g_x3half = Gd()
        g_wpanD = [Gd(), Gd()]
        g_ffps = [Gd(), Gd()]
        g_p45D = Gd()
        g_ttps = Gd()
        g_ttbuf = [Gd(), Gd()]
        g_x2rbuf = [Gd(), Gd()]
        g_x4buf = [Gd(), Gd()]
        g_sqsD = [Gd(), Gd()]
        g_ffree = Gd()

        def d_abld(g, G_CX3=[g.snap() for g in G_CX3]):
            W(g, G_CX3)
            g.dma_start(D_aT, aTi.rearrange("(a p) c -> p a c", p=128)).then_inc(SEM["dab"], 16)
            g.dma_start(D_bB, bBi.rearrange("p (c o) -> p c o", c=NDC)).then_inc(SEM["dab"], 16)
        if RUN_D:
            em("gpsimd", d_abld)
            inc("dab", 32)
        g_abd = gd("dab")

        for hf in range(2 if RUN_D else 0):
            gx3h = g_x3half.snap()
            gcx3 = [g.snap() for g in G_CX3] if hf == 0 else None
            def d_x3ld(sy, hf=hf, gx3h=gx3h, gcx3=gcx3):
                if gcx3 is not None:
                    W(sy, gcx3)
                W(sy, gx3h)
                sy.dma_start(D_x3, x3sc[:, hf * 1024:(hf + 1) * 1024]
                             .rearrange("(c p) t -> p c t", p=128)).then_inc(SEM["dx3"], 16)
            em("sync", d_x3ld)
            inc("dx3", 16)
            g_x3d = gd("dx3")

            for j in range(2):
                gtt = g_ttbuf[j].snap()
                gt7 = g_ttps.snap()
                def d_la(t, j=j, g_x3d=g_x3d, g_abd=g_abd, gt7=gt7):
                    W(t, g_x3d)
                    W(t, g_abd)
                    W(t, gt7)
                    for dc in range(NDC):
                        t.matmul(P7[:], D_aT[:, dc, :], D_x3[:, dc, j * TW:(j + 1) * TW],
                                 start=(dc == 0), stop=(dc == NDC - 1))
                    t.fin(SEM["ten_c"], 1)
                em("tensor", d_la)
                inc("ten_c", 1)
                g_lad = gd("ten_c")

                def d_lev(v, j=j, g_lad=g_lad, gtt=gtt, G_RSC=G_RSC.snap()):
                    W(v, g_lad)
                    W(v, gtt)
                    W(v, G_RSC)
                    v.tensor_scalar(D_tt[j], P7[:], rsc_s[:, 0:1], None, ALU.mult)
                    v.fin(SEM["vec_c"], 1)
                em("vector", d_lev)
                inc("vec_c", 1)
                g_ttps = gd("vec_c")
                g_ttbuf[j] = Gd()

            g_ttd = gd("vec_c")

            for oc in range(NDC):
                gwp = g_wpanD[oc % 2].snap()
                def d_wld(g, hf=hf, oc=oc, gwp=gwp, gwc=(G_WCV.snap() if (hf == 0 and oc <= 1) else None)):
                    W(g, gwp)
                    if gwc is not None:
                        W(g, gwc)
                    g.dma_start(D_wp[oc % 2], wFbf[oc].rearrange("(a p) c -> p a c", p=128)).then_inc(SEM["dw%d" % (oc % 2)], 16)
                em("gpsimd", d_wld)
                inc("dw%d" % (oc % 2), 16)
                g_wd = [gd("dw%d" % (oc % 2)), gd("iw")]

                banks = [PS[0], PS[1]] if oc % 2 == 0 else [PS[2], PS[3]]
                gpf = g_ffps[oc % 2].snap()
                def d_mm(t, oc=oc, banks=banks, g_wd=g_wd, g_ttd=g_ttd, gpf=gpf):
                    W(t, g_wd)
                    W(t, g_ttd)
                    W(t, gpf)
                    for dc in range(NDC):
                        for j in range(2):
                            t.matmul(banks[j][:], D_wp[oc % 2][:, dc, :],
                                     D_x3[:, dc, j * TW:(j + 1) * TW],
                                     start=(dc == 0), stop=False)
                    for j in range(2):
                        t.matmul(banks[j][:], D_bB[:, oc, :], D_tt[j],
                                 start=False, stop=True)
                    t.fin(SEM["ten_c"], 1)
                em("tensor", d_mm)
                inc("ten_c", 1)
                g_wpanD[oc % 2] = gd("ten_c")
                g_mmd = gd("ten_c")

                gff = g_ffree.snap() if (hf == 1 and oc == 0) else None
                def d_ev(v, oc=oc, banks=banks, g_mmd=g_mmd, gff=gff):
                    W(v, g_mmd)
                    if gff is not None:
                        W(v, gff)
                    for j in range(2):
                        v.tensor_scalar(D_ff[:, oc, j * TW:(j + 1) * TW], banks[j][:],
                                        bf_s[:, oc:oc + 1], None, ALU.add)
                    v.fin(SEM["vec_c"], 1)
                em("vector", d_ev)
                inc("vec_c", 1)
                g_ffps[oc % 2] = gd("vec_c")

            g_evall = gd("vec_c")
            g_ttbuf[0] = gd("ten_c")
            g_ttbuf[1] = gd("ten_c")

            for lt in range(2):
                tglob = hf * 2 + lt
                gp45 = g_p45D.snap()
                for dc in range(NDC):
                    gsq = g_sqsD[dc % 2].snap()
                    def d_sq(s, dc=dc, lt=lt, g_evall=g_evall, gsq=gsq):
                        if dc == 0:
                            W(s, g_evall)
                        W(s, gsq)
                        s.activation(D_sqs[dc % 2], D_ff[:, dc, lt * TW:(lt + 1) * TW], AF.Square)
                        s.fin(SEM["sca_c"], 1)
                    em("scalar", d_sq)
                    inc("sca_c", 1)
                    g_sqd = gd("sca_c")

                    def d_st(t, dc=dc, lt=lt, g_sqd=g_sqd, gp45=(gp45 if dc == 0 else None)):
                        W(t, g_sqd)
                        if gp45 is not None:
                            W(t, gp45)
                        t.matmul(P4[:], ones_m[:], D_ff[:, dc, lt * TW:(lt + 1) * TW],
                                 start=(dc == 0), stop=(dc == NDC - 1))
                        t.matmul(P5[:], ones_m[:], D_sqs[dc % 2],
                                 start=(dc == 0), stop=(dc == NDC - 1)).then_inc(SEM["ten_c"], 1)
                    em("tensor", d_st)
                    inc("ten_c", 1)
                    g_sqsD[dc % 2] = gd("ten_c")

                g_st4 = gd("ten_c")
                def d_mid(v, g_st4=g_st4):
                    W(v, g_st4)
                    v.tensor_scalar(D_mu, P4[:], 1.0 / HID, None, ALU.mult)
                    v.tensor_scalar(D_ms, P5[:], 1.0 / HID, None, ALU.mult)
                    v.tensor_mul(D_rs, D_mu, D_mu)
                    v.tensor_sub(D_rs, D_ms, D_rs)
                    v.fin(SEM["vec_c"], 1)
                em("vector", d_mid)
                inc("vec_c", 1)
                g_p45D = gd("vec_c")
                g_mid4 = gd("vec_c")

                def d_rstd(s, g_mid4=g_mid4):
                    W(s, g_mid4)
                    s.activation(D_rs, D_rs, AF.Ln, bias=eps_s[:, 0:1])
                    s.activation(D_rs, D_rs, AF.Exp, scale=-0.5)
                    s.fin(SEM["sca_c"], 1)
                em("scalar", d_rstd)
                inc("sca_c", 1)
                g_rstd4 = gd("sca_c")

                for dc in range(NDC):
                    xb = dc % 2
                    gx2 = g_x2rbuf[xb].snap()
                    def d_x2ld(sy, dc=dc, xb=xb, tglob=tglob, gx2=gx2):
                        W(sy, gx2)
                        sy.dma_start(D_x2r[xb], x2sc[dc * 128:(dc + 1) * 128,
                                                     tglob * TW:(tglob + 1) * TW]).then_inc(SEM["d2%d" % xb], 16)
                    em("sync", d_x2ld)
                    inc("d2%d" % xb, 16)
                    g_x2d = gd("d2%d" % xb)

                    gx4 = g_x4buf[xb].snap()
                    def d_x4(v, dc=dc, xb=xb, lt=lt, g_x2d=g_x2d, g_rstd4=g_rstd4, gx4=gx4):
                        if dc == 0:
                            W(v, g_rstd4)
                        W(v, g_x2d)
                        W(v, gx4)
                        v.tensor_sub(D_f1, D_ff[:, dc, lt * TW:(lt + 1) * TW], D_mu)
                        v.tensor_mul(D_f1, D_f1, D_rs)
                        v.tensor_add(D_x4[xb], D_f1, D_x2r[xb])
                        v.fin(SEM["vec_c"], 1)
                    em("vector", d_x4)
                    inc("vec_c", 1)
                    g_x2rbuf[xb] = gd("vec_c")
                    g_x4d = gd("vec_c")

                    def d_ost(sy, dc=dc, xb=xb, tglob=tglob, g_x4d=g_x4d):
                        W(sy, g_x4d)
                        sy.dma_start(outT[dc * 128:(dc + 1) * 128, tglob * TW:(tglob + 1) * TW],
                                     D_x4[xb]).then_inc(SEM["do%d" % xb], 16)
                    em("sync", d_ost)
                    inc("do%d" % xb, 16)
                    g_x4buf[xb] = gd("do%d" % xb)

            g_x3half = gd("ten_c")
            g_ffree = gd("vec_c")

        if dbg:
            G_ALL = [gd(n) for n in ["do0", "do1", "as0", "as1", "bo0", "bo1",
                                     "cxs", "c30", "c31", "am"]]
            def d_dbg(g, G_ALL=[g.snap() for g in G_ALL]):
                W(g, G_ALL)
                g.dma_start(dbg_outs["dqkv"], qkvT).then_inc(SEM["ddb"], 16)
                if RUN_C:
                    g.dma_start(dbg_outs["dag"], ag_out.ap()).then_inc(SEM["ddb"], 16)
                    g.dma_start(dbg_outs["dx2"], x2sc).then_inc(SEM["ddb"], 16)
                    g.dma_start(dbg_outs["dx3"], x3sc).then_inc(SEM["ddb"], 16)
                    g.dma_start(dbg_outs["drt"], rsc_s[:]).then_inc(SEM["ddb"], 16)
            em("gpsimd", d_dbg)
            inc("ddb", 80 if RUN_C else 16)

        G_FIN = [gd("do0"), gd("do1"), gd("ddb")]
        def fin(sy, G_FIN=[g.snap() for g in G_FIN]):
            W(sy, G_FIN)
        em("sync", fin)

        @block.sync
        def _(sy):
            p = _EngProxy(sy)
            for fn in Q["sync"]:
                fn(p)

        @block.gpsimd
        def _(g):
            p = _EngProxy(g)
            for fn in Q["gpsimd"]:
                fn(p)

        @block.tensor
        def _(t):
            p = _EngProxy(t)
            for fn in Q["tensor"]:
                fn(p)

        @block.vector
        def _(v):
            p = _EngProxy(v, SEM["vbar"])
            for fn in Q["vector"]:
                fn(p)

        @block.scalar
        def _(s):
            p = _EngProxy(s, SEM["sbar"])
            for fn in Q["scalar"]:
                fn(p)

    return nc


def _alibi_slopes32():
    import math
    return np.array([2.0 ** (-(i + 1) / 4.0) for i in range(32)], dtype=np.float32)


def _prep_core_inputs(c, x, w_attn, b_attn, w_proj, b_proj, W, bff, A, B, rW, rb):
    b, g = c // 2, c % 2
    hset = _hset_host(g)
    xTb = np.ascontiguousarray(x[b].T)
    xTown = np.ascontiguousarray(xTb[:, g * 2048:(g + 1) * 2048])

    qcols = np.concatenate([np.arange(h * 128, (h + 1) * 128) for h in hset])
    cols = np.concatenate([qcols, 4096 + qcols, 8192 + qcols])
    wq = np.ascontiguousarray(w_attn[:, cols])
    wq = np.ascontiguousarray(wq.reshape(4096, 48, 128).transpose(1, 0, 2))
    bq = b_attn[cols]
    bqpm = np.ascontiguousarray(bq.reshape(48, 128).T)

    perm = np.concatenate([np.arange(h * 128, (h + 1) * 128)
                           for h in _hset_host(0) + _hset_host(1)])
    wp = np.ascontiguousarray(w_proj[perm, :])
    bppm = np.ascontiguousarray(b_proj.reshape(32, 128).T)

    wft = np.ascontiguousarray(W.T.reshape(4096, 32, 128).transpose(1, 0, 2))
    bfpm = np.ascontiguousarray(bff.reshape(32, 128).T)

    aT = np.ascontiguousarray(A.transpose(2, 0, 1).reshape(4096, 128))
    bB = np.ascontiguousarray(B.transpose(0, 2, 1).reshape(128, 4096))
    wrt = np.ascontiguousarray(rW.T)
    brr = np.ascontiguousarray(rb.reshape(1, 8))

    kvec = (np.arange(32)[None, :] * 128 + np.arange(128)[:, None]).astype(np.float32)
    qvec = np.arange(4096, dtype=np.float32).reshape(1, 4096)
    sl = _alibi_slopes32()
    slp = np.ascontiguousarray(np.broadcast_to(sl[hset][None, :], (128, 16))).astype(np.float32)
    nslp = (-sl[hset]).reshape(1, 16).astype(np.float32)
    ii, jj = np.meshgrid(np.arange(128), np.arange(512), indexing="ij")
    mdel = np.zeros((4, 128, 512), dtype=np.float32)
    for dd in range(4):
        mdel[dd][(ii + 128 * dd) > jj] = -1e30
    er8 = np.zeros((8, 128), dtype=np.float32)
    for e in range(8):
        er8[e, e * 16:(e + 1) * 16] = 1.0
    rmask = np.zeros((1, 4096), dtype=np.float32)
    rmask[0, 611:4095] = 1.0
    rmask = np.ascontiguousarray(rmask[:, g * 2048:(g + 1) * 2048])

    return {
        "xT": xTb, "xTown": xTown, "wQKV": wq, "bQKVpm": bqpm,
        "wP": wp, "bPpm": bppm, "wFT": wft, "bFpm": bfpm,
        "aTi": aT, "bBi": bB, "wRT": wrt, "bRi": brr,
        "kveci": kvec, "qveci": qvec, "slpi": slp, "nslpi": nslp,
        "mdeli": mdel, "er8i": er8, "rmaski": rmask,
    }


def _hset_host(g):
    return list(range(0, 8)) + list(range(16, 24)) if g == 0 else \
           list(range(8, 16)) + list(range(24, 32))


def kernel(L0_attn_hidden_states, L0_attn_w_attn, L0_attn_b_attn, L0_attn_w_proj,
           L0_attn_b_proj, L0_ffn_W, L0_ffn_b, L0_ffn_A, L0_ffn_B,
           L0_ffn_router_W, L0_ffn_router_b):
    x = np.asarray(L0_attn_hidden_states, dtype=np.float32)
    args = (x, np.asarray(L0_attn_w_attn, np.float32), np.asarray(L0_attn_b_attn, np.float32),
            np.asarray(L0_attn_w_proj, np.float32), np.asarray(L0_attn_b_proj, np.float32),
            np.asarray(L0_ffn_W, np.float32), np.asarray(L0_ffn_b, np.float32),
            np.asarray(L0_ffn_A, np.float32), np.asarray(L0_ffn_B, np.float32),
            np.asarray(L0_ffn_router_W, np.float32), np.asarray(L0_ffn_router_b, np.float32))
    nc = build_program(dbg=0)
    in_maps = [_prep_core_inputs(c, *args) for c in range(8)]
    res = run_bass_kernel_spmd(nc, in_maps, core_ids=list(range(8)))
    out = np.empty((4, 4096, 4096), dtype=np.float32)
    for c in range(8):
        b, g = c // 2, c % 2
        out[b, g * 2048:(g + 1) * 2048, :] = res.results[c]["outT"].T
    return out
